# version 13
# baseline (speedup 1.0000x reference)
import numpy as np
import concourse.bass as bass
import concourse.mybir as mybir
from concourse.bass_utils import run_bass_kernel_spmd
from contextlib import ExitStack

F32 = mybir.dt.float32
BF16 = mybir.dt.bfloat16
I8 = mybir.dt.int8
AF = mybir.ActivationFunctionType
ALU = mybir.AluOpType
AX = mybir.AxisListType

D = 1024
T = 4096
TOWN = 2048
HD = 64
INNER = 4096
EPS_LN = 1e-5
EPS_AT = 1e-6
QSCALE = HD ** -0.5
CLIPW = 1.4999999

ENGINES = ("sync", "tensor", "vector", "scalar", "gpsimd")


class Prog:
    def __init__(self):
        self.ops = {e: [] for e in ENGINES}
        self.cnt = {}

    def newsem(self, name):
        self.cnt[name] = 0

    def add(self, eng, fn=None, wait=(), inc=None, n=1):
        out = None
        incs = ()
        if inc is not None:
            self.cnt[inc] += n
            out = self.cnt[inc]
            incs = ((inc, n),)
        wait = tuple((s, v) for (s, v) in wait if v and v > 0)
        self.ops[eng].append((fn, wait, incs))
        return out

    def val(self, sem):
        return self.cnt[sem]

    def emit(self, block, sems):
        def mk(eng_name):
            ops = self.ops[eng_name]

            def body(eng):
                for fn, waits, incs in ops:
                    for s, v in waits:
                        eng.wait_ge(sems[s], v)
                    instr = fn(eng) if fn is not None else None
                    for s, n in incs:
                        instr.then_inc(sems[s], n)
            return body
        block.sync(mk("sync"))
        block.tensor(mk("tensor"))
        block.vector(mk("vector"))
        block.scalar(mk("scalar"))
        block.gpsimd(mk("gpsimd"))
        self.ops = {e: [] for e in ENGINES}


def build_nc():
    nc = bass.Bass("TRN2", target_bir_lowering=False, debug=False)

    xp = nc.dram_tensor("xp", [T, D], F32, kind="ExternalInput").ap()
    wqT = nc.dram_tensor("wqT", [D, D], F32, kind="ExternalInput").ap()
    wkT = nc.dram_tensor("wkT", [D, D], F32, kind="ExternalInput").ap()
    wvT = nc.dram_tensor("wvT", [D, D], F32, kind="ExternalInput").ap()
    woT = nc.dram_tensor("woT", [D, D], F32, kind="ExternalInput").ap()
    wgTd = nc.dram_tensor("wgT", [D, INNER], F32, kind="ExternalInput").ap()
    wuTd = nc.dram_tensor("wuT", [D, INNER], F32, kind="ExternalInput").ap()
    wdTd = nc.dram_tensor("wdT", [INNER, D], F32, kind="ExternalInput").ap()
    mpin = nc.dram_tensor("mpart", [128, 128], F32, kind="ExternalInput").ap()
    yout = nc.dram_tensor("yout", [TOWN, D], F32, kind="ExternalOutput").ap()
    x2d = nc.dram_tensor("x2d", [TOWN, D], F32, kind="Internal")

    xp_t = xp.rearrange("(t p) d -> t p d", p=128)
    yout_t = yout.rearrange("(t p) d -> t p d", p=128)
    x2d_t = x2d.ap().rearrange("(t p) d -> t p d", p=128)
    wq_r = wqT.rearrange("(c p) f -> p c f", p=128)
    wk_r = wkT.rearrange("(c p) f -> p c f", p=128)
    wv_r = wvT.rearrange("(c p) f -> p c f", p=128)
    wo_r = woT.rearrange("(c p) f -> p c f", p=128)
    wg_r = wgTd.rearrange("(c p) f -> c p f", p=128)
    wu_r = wuTd.rearrange("(c p) f -> c p f", p=128)
    wd_r = wdTd.rearrange("(c p) f -> c p f", p=128)

    P = Prog()
    SEMS = ("dS", "dG", "tP", "vV", "sS", "dA", "dB", "dW0", "dW1",
            "dR0", "dR1", "dX")
    for s in SEMS:
        P.newsem(s)

    top = ExitStack()
    E = top.enter_context
    sems = {s: E(nc.semaphore(s)) for s in SEMS}

    ident = E(nc.sbuf_tensor("ident", [128, 128], BF16))
    mtri = E(nc.sbuf_tensor("mtri", [128, 128], BF16))
    mpart = E(nc.sbuf_tensor("mpart_sb", [128, 128], BF16))
    ones_r = E(nc.sbuf_tensor("ones_r", [1, 128], BF16))
    ones_c = E(nc.sbuf_tensor("ones_c", [128, 1], BF16))
    ones_cf = E(nc.sbuf_tensor("ones_cf", [128, 1], F32))
    ones_rf = E(nc.sbuf_tensor("ones_rf", [1, 128], F32))
    wstat = E(nc.sbuf_tensor("wstat", [128, 96], F32))
    wscl = E(nc.sbuf_tensor("wscl", [128, 8], F32))
    small = E(nc.sbuf_tensor("small", [128, 64], F32))
    rsx = E(nc.sbuf_tensor("rsx", [128, 16], F32))
    sgc = E(nc.sbuf_tensor("sgc", [128, 16], F32))
    cst = E(nc.sbuf_tensor("cst", [128, 2], F32))
    ones_sq = E(nc.sbuf_tensor("ones_sq", [128, 128], F32))

    abc = ExitStack()
    EB = abc.enter_context
    iof = EB(nc.sbuf_tensor("iof", [128, 128], BF16))
    iop = EB(nc.sbuf_tensor("iop", [128, 128], BF16))
    xn_fm = EB(nc.sbuf_tensor("xn_fm", [128, 8 * T], BF16))
    OF = EB(nc.sbuf_tensor("OF", [128, 8 * TOWN], BF16))
    K_sb = EB(nc.sbuf_tensor("K_sb", [128, 2 * T], BF16))
    Q_sb = EB(nc.sbuf_tensor("Q_sb", [128, 2 * TOWN], BF16))
    V1 = EB(nc.sbuf_tensor("V1", [128, 32 * 260], BF16))
    Pbuf = EB(nc.sbuf_tensor("Pbuf", [128, 4 * 512], BF16))
    wq_sl = EB(nc.sbuf_tensor("wq_sl", [128, 8 * 256], BF16))
    wk_sl = EB(nc.sbuf_tensor("wk_sl", [128, 8 * 256], BF16))
    wv_sl = EB(nc.sbuf_tensor("wv_sl", [128, 8 * 256], BF16))
    zrow = EB(nc.sbuf_tensor("zrow", [1, 2 * 512], BF16))
    vcs = EB(nc.sbuf_tensor("vcs", [1, 16 * 260], BF16))
    rz_sb = EB(nc.sbuf_tensor("rz_sb", [128, 512], F32))
    xld = [EB(nc.sbuf_tensor(f"xld{k}", [128, D], F32)) for k in range(2)]
    xnt = [EB(nc.sbuf_tensor(f"xnt{k}", [128, D], BF16)) for k in range(2)]
    st_s = EB(nc.sbuf_tensor("st_s", [128, 32], F32))
    st_m = EB(nc.sbuf_tensor("st_m", [128, 32], F32))
    st_q = EB(nc.sbuf_tensor("st_q", [128, 32], F32))
    st_r = EB(nc.sbuf_tensor("st_r", [128, 32], F32))
    wmb = [EB(nc.sbuf_tensor(f"wmb{k}", [128, 1024], F32)) for k in range(2)]
    xres = [EB(nc.sbuf_tensor(f"xres{k}", [128, D], F32)) for k in range(2)]
    x2t = [EB(nc.sbuf_tensor(f"x2t{k}", [128, D], F32)) for k in range(2)]

    psABC = ExitStack()
    EP = psABC.enter_context
    ps_s = [EP(nc.psum_tensor(f"ps_s{i}", [128, 512], F32)) for i in range(4)]
    ps_o = [EP(nc.psum_tensor(f"ps_o{i}", [65, 512], F32)) for i in range(2)]
    ps_v = EP(nc.psum_tensor("ps_v", [1, 260], F32))
    ps_y = EP(nc.psum_tensor("ps_y", [128, 512], F32))

    sbank_sS = [0, 0, 0, 0]
    sbank_vV = [0, 0, 0, 0]
    obank_free = [0, 0]

    P.add("gpsimd", lambda g: g.iota(iof[:], [[1, 128]], channel_multiplier=0,
                                     allow_small_or_imprecise_dtypes=True))
    P.add("gpsimd", lambda g: g.iota(iop[:], [[0, 128]], channel_multiplier=1,
                                     allow_small_or_imprecise_dtypes=True),
          inc="dG", n=16)
    v_io = P.val("dG")
    v_mp = P.add("gpsimd", lambda g: g.dma_start(out=mpart[:], in_=mpin), inc="dG", n=16)
    v_cst = P.add("vector", lambda v: v.memset(cst[:, 0:1], EPS_LN), inc="vV")
    P.add("vector", lambda v: v.memset(ones_sq[:], 1.0), inc="vV")
    P.add("vector", lambda v: v.memset(ones_r[:], 1.0), inc="vV")
    P.add("vector", lambda v: v.memset(ones_c[:], 1.0), inc="vV")
    P.add("vector", lambda v: v.memset(ones_cf[:], 1.0), inc="vV")
    P.add("vector", lambda v: v.memset(ones_rf[:], 1.0), inc="vV")
    P.add("vector", lambda v: v.tensor_tensor(ident[:], iof[:], iop[:], ALU.is_equal),
          wait=[("dG", v_io)], inc="vV")
    v_const_all = P.add("vector", lambda v: v.tensor_tensor(mtri[:], iop[:], iof[:], ALU.is_le),
                        inc="vV")

    wsl_ld_g0 = None
    for (dst, src) in ((wq_sl, wq_r), (wk_sl, wk_r), (wv_sl, wv_r)):
        wsl_ld_g0 = P.add("gpsimd", lambda gg, dst=dst, src=src: gg.dma_start(
            out=dst[:].rearrange("p (c f) -> p c f", f=256),
            in_=src[:, :, 0:256]), inc="dG", n=16)
    v_mp_safe = wsl_ld_g0

    V1r = V1[:].rearrange("p (s c) -> p s c", c=260)
    v_v1ones = None
    for h in range(4):
        v_v1ones = P.add("vector", lambda v, h=h: v.memset(
            V1r[:, :, 65 * h + 64:65 * h + 65], 1.0), inc="vV")

    bank_rr = [0]
    v1_guarded = [False]

    def proj_tile(lhs_sl, wcol, rhs_off, width, first_w, ev_fn, ev_eng):
        b = bank_rr[0] % 4
        bank_rr[0] += 1
        last = None
        for ic in range(8):
            w = []
            if ic == 0:
                w = [("sS", sbank_sS[b]), ("vV", sbank_vV[b])] + first_w
            last = P.add("tensor", lambda tt, ic=ic, b=b, lhs_sl=lhs_sl, wcol=wcol,
                         rhs_off=rhs_off, width=width: tt.matmul(
                ps_s[b][:, 0:width],
                lhs_sl[:, ic * 256 + wcol: ic * 256 + wcol + 128],
                xn_fm[:, ic * T + rhs_off: ic * T + rhs_off + width],
                start=(ic == 0), stop=(ic == 7), skip_group_check=True),
                wait=w, inc=("tP" if ic == 7 else None))
        ev = P.add(ev_eng, lambda e, b=b: ev_fn(e, ps_s[b]),
                   wait=[("tP", last)],
                   inc=("sS" if ev_eng == "scalar" else "vV"))
        if ev_eng == "scalar":
            sbank_sS[b], sbank_vV[b] = ev, 0
        else:
            sbank_sS[b], sbank_vV[b] = 0, ev

    def emit_vchain(t1, extra_w=()):
        b = bank_rr[0] % 4
        bank_rr[0] += 1
        last = None
        for ic in range(8):
            w = []
            if ic == 0:
                w = [("sS", sbank_sS[b]), ("vV", sbank_vV[b])] + list(extra_w)
            last = P.add("tensor", lambda tt, ic=ic, b=b, t1=t1: tt.matmul(
                ps_s[b][:, 0:256],
                xn_fm[:, ic * T + t1 * 128: ic * T + t1 * 128 + 128],
                wv_sl[:, ic * 256: ic * 256 + 256],
                start=(ic == 0), stop=(ic == 7), skip_group_check=True),
                wait=w, inc=("tP" if ic == 7 else None))
        evv = None
        for h in range(4):
            w = [("tP", last)] if h == 0 else []
            if not v1_guarded[0] and h == 0:
                v1_guarded[0] = True
                w.append(("vV", v_v1ones))
            evv = P.add("vector", lambda v, b=b, t1=t1, h=h: v.tensor_copy(
                V1[:, t1 * 260 + 65 * h: t1 * 260 + 65 * h + 64],
                ps_s[b][:, h * 64:(h + 1) * 64]),
                wait=w, inc="vV")
        sbank_sS[b], sbank_vV[b] = 0, evv

    def emit_block_projs(t8):
        xn_done = P.val("sS")
        fw0 = [("sS", xn_done)]
        if t8 == 0:
            fw0 = fw0 + [("dG", wsl_ld_g0)]
        for kc in range(2):
            dsto = kc * T + t8 * 512
            fw = fw0 if kc == 0 else []
            if (t8 * 2 + kc) % 2 == 0:
                proj_tile(wk_sl, kc * 128, t8 * 512, 512, fw,
                          lambda e, psb, dsto=dsto: e.copy(
                              K_sb[:, dsto:dsto + 512], psb[:, 0:512]), "scalar")
            else:
                proj_tile(wk_sl, kc * 128, t8 * 512, 512, fw,
                          lambda e, psb, dsto=dsto: e.tensor_copy(
                              K_sb[:, dsto:dsto + 512], psb[:, 0:512]), "vector")
        if t8 < 4:
            for kc in range(2):
                dsto = kc * TOWN + t8 * 512
                proj_tile(wq_sl, kc * 128, t8 * 512, 512, [],
                          lambda e, psb, dsto=dsto: e.tensor_scalar(
                              Q_sb[:, dsto:dsto + 512], psb[:, 0:512],
                              QSCALE, None, ALU.mult), "vector")
        for t1 in range(4 * t8, 4 * t8 + 4):
            emit_vchain(t1)

    xn_ready = [0] * 2
    xn_freed = 0
    xld_freed = [0, 0]
    stats_state = {}
    vchain = [0]

    def vop(fn, wait=()):
        w = list(wait) + [("vV", vchain[0])]
        vchain[0] = P.add("vector", fn, wait=w, inc="vV")
        return vchain[0]


    def emit_stats(t):
        kx = t % 2
        k4 = t % 2
        sm = "dA" if t % 2 == 0 else "dB"
        ld = P.add("sync", lambda s, kx=kx, t=t: s.dma_start(
            out=xld[kx][:], in_=xp_t[t]),
            wait=[("vV", xld_freed[kx])], inc=sm, n=16)
        vop(lambda v, kx=kx, t=t: v.tensor_reduce(
            st_s[:, t:t + 1], xld[kx][:], axis=AX.X, op=ALU.add),
            wait=[(sm, ld)])
        vop(lambda v, t=t: v.tensor_scalar(
            st_m[:, t:t + 1], st_s[:, t:t + 1], -1.0 / D, None, ALU.mult))
        v_ctr = vop(lambda v, kx=kx, k4=k4, t=t: v.tensor_scalar(
            xnt[k4][:], xld[kx][:], st_m[:, t:t + 1], None, ALU.add),
            wait=[("tP", xn_freed)])
        xld_freed[kx] = v_ctr
        s_sq = P.add("scalar", lambda s, k4=k4, t=t: s.activation(
            wmb[0][:], xnt[k4][:], AF.Square, accum_out=st_q[:, t:t + 1]),
            wait=[("vV", v_ctr)], inc="sS")
        stats_state[t] = (kx, s_sq, 0)

    def emit_finish(t):
        kx, s_sq, _ = stats_state.pop(t)
        k4 = t % 2
        v_var = vop(lambda v, t=t: v.tensor_scalar(
            st_q[:, t:t + 1], st_q[:, t:t + 1], 1.0 / D, None, ALU.mult),
            wait=[("sS", s_sq)])
        s_std = P.add("scalar", lambda s, t=t: s.activation(
            st_s[:, t:t + 1], st_q[:, t:t + 1], AF.Sqrt, bias=cst[:, 0:1]),
            wait=[("vV", max(v_var, v_cst))], inc="sS")
        v_rec = vop(lambda v, t=t: v.reciprocal(
            st_r[:, t:t + 1], st_s[:, t:t + 1]),
            wait=[("sS", s_std)])
        xn_ready[k4] = P.add("scalar", lambda s, k4=k4, t=t: s.activation(
            xnt[k4][:], xnt[k4][:], AF.Copy, scale=st_r[:, t:t + 1]),
            wait=[("vV", v_rec)], inc="sS")

    def emit_transposes(tg):
        for f in range(8):
            b = f % 2
            last = None
            for k4 in range(2):
                w = ([("vV", max(v_const_all, sbank_vV[b])),
                      ("sS", max(xn_ready + [sbank_sS[b]]))] if k4 == 0 else [])
                last = P.add("tensor", lambda tt, f=f, k4=k4, b=b: tt.matmul(
                    ps_s[b][:, k4 * 128:(k4 + 1) * 128],
                    xnt[k4][:, f * 128:(f + 1) * 128], ident[:],
                    start=(k4 == 0), stop=(k4 == 1), skip_group_check=True),
                    wait=w, inc=("tP" if k4 == 1 else None))
            sbank_sS[b] = P.add("scalar", lambda s, f=f, tg=tg, b=b: s.copy(
                xn_fm[:, f * T + tg * 256: f * T + tg * 256 + 256],
                ps_s[b][:, 0:256]),
                wait=[("tP", last)], inc="sS")
        return P.val("tP")

    for idx in range(33):
        if idx >= 1:
            t = idx - 1
            emit_finish(t)
            if t % 2 == 1:
                tg = t // 2
                xn_freed = emit_transposes(tg)
                if tg % 2 == 1:
                    emit_block_projs(tg // 2)
        if idx < 32:
            emit_stats(idx)
    xnfm_all = P.val("sS")
    v_projA_sS = P.val("sS")
    v_projA_vV = P.val("vV")

    wmean_tiles = []
    for (rr, nch) in ((wg_r, 8), (wu_r, 8), (wd_r, 32)):
        for ci in range(nch):
            nb = 4 if rr is not wd_r else 1
            for fb in range(nb):
                wmean_tiles.append((rr, ci, fb * 1024))
    assert len(wmean_tiles) == 96
    wm_abs = {}

    def emit_wmean():
        idx = len(wm_abs)
        if idx >= 96:
            return
        rr, ci, f0 = wmean_tiles[idx]
        buf = wmb[idx % 2]
        sm = "dW0" if idx % 2 == 0 else "dW1"
        w = [("vV", wm_abs[idx - 2])] if idx >= 2 else [("sS", xnfm_all)]
        ld = P.add("sync", lambda s, rr=rr, ci=ci, f0=f0, buf=buf: s.dma_start(
            out=buf[:], in_=rr[ci][:, f0:f0 + 1024]), wait=w, inc=sm, n=16)
        wm_abs[idx] = P.add("vector", lambda v, buf=buf, idx=idx: v.tensor_reduce(
            wstat[:, idx:idx + 1], buf[:], axis=AX.X, op=ALU.add,
            apply_absolute_value=True),
            wait=[(sm, ld)], inc="vV")

    bank_rr = [0]
    prev_av_last = [0]
    pending_norm = []
    norm_done = {}
    zslot_free = [0]
    zb_free = [0]

    def emit_norm():
        if not pending_norm:
            return
        cp, J, i2, e4 = pending_norm.pop(0)
        P.add("tensor", lambda tt, i2=i2: tt.matmul(
            ps_y[0:64, :], ones_r[0:1, 0:64],
            zrow[0:1, (2 * i2) * 512:(2 * i2) * 512 + 512],
            start=True, stop=False, skip_group_check=True),
            wait=[("sS", e4), ("vV", zb_free[0])])
        mmz = P.add("tensor", lambda tt, i2=i2: tt.matmul(
            ps_y[64:128, :], ones_r[0:1, 0:64],
            zrow[0:1, (2 * i2 + 1) * 512:(2 * i2 + 1) * 512 + 512],
            start=True, stop=True, skip_group_check=True), inc="tP")
        zslot_free[i2] = mmz
        zb_free[0] = P.add("vector", lambda v: v.tensor_scalar(
            rz_sb[:], ps_y[:], 2 * EPS_AT, None, ALU.add),
            wait=[("tP", mmz)], inc="vV")
        P.add("vector", lambda v: v.reciprocal(rz_sb[:], rz_sb[:]))
        norm_done[(cp, J)] = P.add("vector", lambda v, cp=cp, J=J: v.tensor_tensor(
            OF[:, cp * TOWN + J * 512: cp * TOWN + J * 512 + 512],
            OF[:, cp * TOWN + J * 512: cp * TOWN + J * 512 + 512],
            rz_sb[:], ALU.mult), inc="vV")

    for g in range(4):
        if g > 0:
            wsl_ld = None
            w0 = [("tP", P.val("tP"))]
            for (dst, src) in ((wq_sl, wq_r), (wk_sl, wk_r), (wv_sl, wv_r)):
                wsl_ld = P.add("gpsimd", lambda gg, dst=dst, src=src, g=g: gg.dma_start(
                    out=dst[:].rearrange("p (c f) -> p c f", f=256),
                    in_=src[:, :, g * 256:(g + 1) * 256]),
                    wait=w0, inc="dG", n=16)
                w0 = []
            ei = 0
            for kc in range(2):
                for t8 in range(8):
                    fw = [("dG", wsl_ld)] if ei == 0 else []
                    dsto = kc * T + t8 * 512
                    if ei % 2 == 0:
                        proj_tile(wk_sl, kc * 128, t8 * 512, 512, fw,
                                  lambda e, psb, dsto=dsto: e.copy(
                                      K_sb[:, dsto:dsto + 512], psb[:, 0:512]), "scalar")
                    else:
                        proj_tile(wk_sl, kc * 128, t8 * 512, 512, fw,
                                  lambda e, psb, dsto=dsto: e.tensor_copy(
                                      K_sb[:, dsto:dsto + 512], psb[:, 0:512]), "vector")
                    ei += 1
            for kc in range(2):
                for t4 in range(4):
                    dsto = kc * TOWN + t4 * 512
                    proj_tile(wq_sl, kc * 128, t4 * 512, 512, [],
                              lambda e, psb, dsto=dsto: e.tensor_scalar(
                                  Q_sb[:, dsto:dsto + 512], psb[:, 0:512],
                                  QSCALE, None, ALU.mult), "vector")
            for t1 in range(32):
                emit_vchain(t1)
                if t1 % 2 == 1:
                    emit_wmean()
            v_proj_sS = P.val("sS")
            v_proj_vV = P.val("vV")
        else:
            v_proj_sS = v_projA_sS
            v_proj_vV = v_projA_vV

        last_copy = 0
        for t in range(1, 16):
            w = ([("vV", v_proj_vV), ("sS", v_proj_sS)] if t == 1
                 else [("sS", last_copy)])
            P.add("tensor", lambda tt, t=t: tt.matmul(
                ps_v[:], ones_c[:], V1[:, (t - 1) * 260:(t - 1) * 260 + 260],
                start=(t == 1), stop=False, skip_group_check=True), wait=w)
            mm2 = P.add("tensor", lambda tt, t=t: tt.matmul(
                ps_v[:], ones_c[:], V1[:, (16 + t - 1) * 260:(16 + t - 1) * 260 + 260],
                start=False, stop=(t == 15), skip_group_check=True), inc="tP")
            last_copy = P.add("scalar", lambda s, t=t: s.copy(
                vcs[0:1, t * 260:(t + 1) * 260], ps_v[:]),
                wait=[("tP", mm2)], inc="sS")
        v_snap = last_copy

        for lp in range(2):
            cp = 2 * g + lp
            for J in range(4):
                for _ in range(3 if g == 0 else 1):
                    emit_wmean()
                slots = list(range(4 * J + 4)) + [16 + r for r in range(4 * J + 4)]
                ns = len(slots)
                pA, pB, sttA, sttB, avB, smm = {}, {}, {}, {}, {}, {}

                def c0_of(sl):
                    return 128 * max(0, (sl % 16) - 4 * J)

                def emit_av(si, lp=lp, J=J, slots=slots, sttA=sttA, sttB=sttB,
                            pA=pA, pB=pB, avB=avB):
                    sl = slots[si]
                    c0 = 128 * max(0, (sl % 16) - 4 * J)
                    bb = si % 2
                    boundary = (sl % 16) >= 4 * J
                    wA = [("vV", sttB[si])] if boundary else [("sS", pA[si])]
                    if si == 0:
                        wA += [("sS", obank_free[0]), ("sS", obank_free[1])]
                    P.add("tensor", lambda tt, sl=sl, c0=c0, bb=bb, si=si, lp=lp: tt.matmul(
                        ps_o[0][:, c0:512],
                        V1[:, sl * 260 + 65 * (2 * lp): sl * 260 + 65 * (2 * lp) + 65],
                        Pbuf[:, (2 * bb) * 512 + c0:(2 * bb) * 512 + 512],
                        start=(si == 0), stop=False, skip_group_check=True),
                        wait=wA)
                    wB = [("vV", sttB[si])] if boundary else [("sS", pB[si])]
                    avB[si] = P.add("tensor", lambda tt, sl=sl, c0=c0, bb=bb, si=si, lp=lp: tt.matmul(
                        ps_o[1][:, c0:512],
                        V1[:, sl * 260 + 65 * (2 * lp + 1): sl * 260 + 65 * (2 * lp + 1) + 65],
                        Pbuf[:, (2 * bb + 1) * 512 + c0:(2 * bb + 1) * 512 + 512],
                        start=(si == 0), stop=False, skip_group_check=True),
                        wait=wB, inc="tP")

                for si, sl in enumerate(slots):
                    c0 = c0_of(sl)
                    bb = si % 2
                    bA, bB = 2 * bb, 2 * bb + 1
                    boundary = (sl % 16) >= 4 * J
                    if si < 2:
                        w = [("sS", max(sbank_sS[bA], sbank_sS[bB], v_snap)),
                             ("vV", max(sbank_vV[bA], sbank_vV[bB]))]
                        sbank_sS[bA] = sbank_sS[bB] = 0
                        sbank_vV[bA] = sbank_vV[bB] = 0
                    else:
                        w = [("sS", pB[si - 2])]
                    kcol = lp * T + sl * 128
                    qcol = lp * TOWN + J * 512 + c0
                    P.add("tensor", lambda tt, bA=bA, c0=c0, kcol=kcol, qcol=qcol: tt.matmul(
                        ps_s[bA][:, c0:512], K_sb[0:64, kcol:kcol + 128],
                        Q_sb[0:64, qcol:qcol + 512 - c0],
                        start=True, stop=True, tile_position=(0, 0),
                        skip_group_check=True), wait=w)
                    smm[si] = P.add("tensor", lambda tt, bB=bB, c0=c0, kcol=kcol, qcol=qcol: tt.matmul(
                        ps_s[bB][:, c0:512], K_sb[64:128, kcol:kcol + 128],
                        Q_sb[64:128, qcol:qcol + 512 - c0],
                        start=True, stop=True, tile_position=(64, 0),
                        skip_group_check=True), inc="tP")
                    if si == 2:
                        emit_norm()
                    if si >= 1:
                        emit_av(si - 1)
                    pw = [("tP", smm[si])]
                    if si >= 2:
                        pw.append(("tP", avB[si - 2]))
                    elif prev_av_last[0]:
                        pw.append(("tP", prev_av_last[0]))
                    pA[si] = P.add("scalar", lambda s, bb=bb, bA=bA, c0=c0: s.activation(
                        Pbuf[:, (2 * bb) * 512 + c0:(2 * bb) * 512 + 512],
                        ps_s[bA][:, c0:512], AF.Square, bias=1.0),
                        wait=pw, inc="sS")
                    pB[si] = P.add("scalar", lambda s, bb=bb, bB=bB, c0=c0: s.activation(
                        Pbuf[:, (2 * bb + 1) * 512 + c0:(2 * bb + 1) * 512 + 512],
                        ps_s[bB][:, c0:512], AF.Square, bias=1.0),
                        inc="sS")
                    if boundary:
                        mk_ = mtri if sl < 16 else mpart
                        mw = [("sS", pA[si])]
                        if sl >= 16:
                            mw.append(("dG", v_mp_safe))
                        sttA[si] = P.add("vector", lambda v, bb=bb, c0=c0, mk_=mk_: v.scalar_tensor_tensor(
                            Pbuf[:, (2 * bb) * 512 + c0:(2 * bb) * 512 + c0 + 128],
                            Pbuf[:, (2 * bb) * 512 + c0:(2 * bb) * 512 + c0 + 128],
                            1.0, mk_[:], ALU.add, ALU.mult), wait=mw, inc="vV")
                        sttB[si] = P.add("vector", lambda v, bb=bb, c0=c0, mk_=mk_: v.scalar_tensor_tensor(
                            Pbuf[:, (2 * bb + 1) * 512 + c0:(2 * bb + 1) * 512 + c0 + 128],
                            Pbuf[:, (2 * bb + 1) * 512 + c0:(2 * bb + 1) * 512 + c0 + 128],
                            1.0, mk_[:], ALU.add, ALU.mult),
                            wait=[("sS", pB[si])], inc="vV")
                emit_av(ns - 1)
                sbank_sS[0] = sbank_sS[1] = pB[ns - 2]
                sbank_sS[2] = sbank_sS[3] = pB[ns - 1]
                prev_av_last[0] = avB[ns - 1]
                lastk = avB[ns - 1]
                for i in range(4):
                    t = 4 * J + i
                    if t == 0:
                        continue
                    for oi, hl in ((0, 2 * lp), (1, 2 * lp + 1)):
                        stop = (i == 3) and (oi == 1)
                        lastk = P.add("tensor", lambda tt, t=t, oi=oi, hl=hl, i=i, stop=stop: tt.matmul(
                            ps_o[oi][:, i * 128:(i + 1) * 128],
                            vcs[0:1, t * 260 + 65 * hl: t * 260 + 65 * hl + 65],
                            ones_r[:, 0:128],
                            start=False, stop=stop, skip_group_check=True),
                            inc=("tP" if stop else None))
                i2 = 0
                P.add("scalar", lambda s, cp=cp, J=J: s.copy(
                    OF[0:64, cp * TOWN + J * 512: cp * TOWN + J * 512 + 512],
                    ps_o[0][0:64, :]), wait=[("tP", lastk)], inc="sS")
                P.add("scalar", lambda s, cp=cp, J=J: s.copy(
                    OF[64:128, cp * TOWN + J * 512: cp * TOWN + J * 512 + 512],
                    ps_o[1][0:64, :]), inc="sS")
                e3 = P.add("scalar", lambda s, i2=i2: s.copy(
                    zrow[0:1, (2 * i2) * 512:(2 * i2) * 512 + 512],
                    ps_o[0][64:65, :]),
                    wait=[("tP", zslot_free[i2])], inc="sS")
                e4 = P.add("scalar", lambda s, i2=i2: s.copy(
                    zrow[0:1, (2 * i2 + 1) * 512:(2 * i2 + 1) * 512 + 512],
                    ps_o[1][64:65, :]), inc="sS")
                obank_free[0], obank_free[1] = e3, e4
                pending_norm.append((cp, J, i2, e4))
    while pending_norm:
        emit_norm()
    while len(wm_abs) < 96:
        emit_wmean()
    v_wmean_all = wm_abs[95]

    v_wo = P.add("gpsimd", lambda gg: gg.dma_start(
        out=K_sb[:].rearrange("p (c f) -> p c f", f=1024), in_=wo_r[:, :, :]),
        wait=[("tP", P.val("tP"))], inc="dG", n=16)

    for mi, (a, bcnt) in enumerate(((0, 32), (32, 32), (64, 32))):
        vop(lambda v, mi=mi, a=a, bcnt=bcnt: v.tensor_reduce(
            small[:, 32 + mi:33 + mi], wstat[:, a:a + bcnt], axis=AX.X, op=ALU.add),
            wait=[("vV", v_wmean_all)])
    vop(lambda v: v.tensor_copy(small[:, 36:39], small[:, 32:35]))
    for _ in range(4):
        vop(lambda v: v.tensor_copy(small[:, 0:1], small[:, 36:37]))
    v_cast = vop(lambda v: v.tensor_copy(small[:, 0:1], small[:, 36:37]))
    mmw = P.add("tensor", lambda tt: tt.matmul(
        ps_y[:, 0:3], ones_sq[:], small[:, 36:39],
        start=True, stop=True, skip_group_check=True),
        wait=[("vV", v_cast)], inc="tP")
    nels = [float(D * INNER), float(D * INNER), float(INNER * D)]
    for mi in range(3):
        vop(lambda v, mi=mi: v.tensor_scalar(
            wscl[:, 3 + mi:4 + mi], ps_y[:, mi:mi + 1], 1.0 / nels[mi],
            None, ALU.mult),
            wait=([("tP", mmw)] if mi == 0 else []))
        vop(lambda v, mi=mi: v.tensor_scalar(
            wscl[:, 3 + mi:4 + mi], wscl[:, 3 + mi:4 + mi], 1e-5, None, ALU.max))
        vop(lambda v, mi=mi: v.reciprocal(wscl[:, mi:mi + 1], wscl[:, 3 + mi:4 + mi]))
    vop(lambda v: v.tensor_tensor(
        wscl[:, 6:7], wscl[:, 4:5], wscl[:, 5:6], ALU.mult))
    v_wscl = vop(lambda v: v.tensor_scalar(
        wscl[:, 6:7], wscl[:, 6:7], 1.0 / 127.0, None, ALU.mult))

    xres_free = [0, 0]
    x2_store = {}
    for tt16 in range(16):
        k = tt16 % 2
        smR = "dR0" if k == 0 else "dR1"
        ld = P.add("sync", lambda s, tt16=tt16, k=k: s.dma_start(
            out=xres[k][:], in_=xp_t[tt16]),
            wait=[("vV", xres_free[k])], inc=smR, n=16)
        evs = []
        for half in range(2):
            b = bank_rr[0] % 4
            bank_rr[0] += 1
            last = None
            for oc in range(8):
                w = []
                if oc == 0:
                    w = [("sS", sbank_sS[b]), ("vV", sbank_vV[b]),
                         ("vV", max(norm_done[(cc, tt16 // 4)] for cc in range(8)))]
                    if tt16 == 0 and half == 0:
                        w.append(("dG", v_wo))
                last = P.add("tensor", lambda tt, oc=oc, b=b, tt16=tt16, half=half: tt.matmul(
                    ps_s[b][:, 0:512],
                    OF[:, oc * TOWN + tt16 * 128: oc * TOWN + tt16 * 128 + 128],
                    K_sb[:, oc * 1024 + half * 512: oc * 1024 + half * 512 + 512],
                    start=(oc == 0), stop=(oc == 7), skip_group_check=True),
                    wait=w, inc=("tP" if oc == 7 else None))
            wv = [("tP", last), (smR, ld)]
            if tt16 >= 2:
                wv.append(("dX", x2_store[tt16 - 1]))
            ev = P.add("vector", lambda v, b=b, half=half, k=k: v.tensor_tensor(
                x2t[k][:, half * 512:(half + 1) * 512], ps_s[b][:, 0:512],
                xres[k][:, half * 512:(half + 1) * 512], ALU.add),
                wait=wv, inc="vV")
            evs.append(ev)
            sbank_sS[b], sbank_vV[b] = 0, ev
        xres_free[k] = evs[-1]
        x2_store[tt16] = P.add("sync", lambda s, tt16=tt16, k=k: s.dma_start(
            out=x2d_t[tt16], in_=x2t[k][:]),
            wait=[("vV", evs[-1])], inc="dX", n=16)

    with nc.Block() as block:
        P.emit(block, sems)
    psABC.close()
    abc.close()

    dstk = ExitStack()
    ED = dstk.enter_context
    xq_fm = ED(nc.sbuf_tensor("xq_fm", [128, 8 * TOWN], BF16))
    hbuf = ED(nc.sbuf_tensor("hbuf", [128, 8 * INNER], BF16))
    wdq = ED(nc.sbuf_tensor("wdq", [128, 32 * 1024], BF16))
    wgqs = [ED(nc.sbuf_tensor(f"wgq{k}", [128, 8 * 512], BF16)) for k in range(2)]
    wfs = [ED(nc.sbuf_tensor(f"wfs{k}", [128, 512], F32)) for k in range(2)]
    wqi = ED(nc.sbuf_tensor("wqi", [128, 512], I8))
    wqi2 = ED(nc.sbuf_tensor("wqi2", [128, 512], I8))
    x2l = [ED(nc.sbuf_tensor(f"x2l{k}", [128, D], F32)) for k in range(2)]
    sqs = ED(nc.sbuf_tensor("sqs", [128, D], F32))
    xqb = ED(nc.sbuf_tensor("xqb", [128, D], BF16))
    hq_fm = [ED(nc.sbuf_tensor("hqfm0", [128, INNER], BF16))] * 2

    psD = ExitStack()
    EPD = psD.enter_context
    pd = [EPD(nc.psum_tensor(f"pd{i}", [128, 512], F32)) for i in range(6)]
    pd_w = EPD(nc.psum_tensor("pd_w", [1, 4], F32))
    pd_b = EPD(nc.psum_tensor("pd_b", [128, 1], F32))
    dstat = wstat
    pdbank_sS = [0] * 6
    pdbank_vV = [0] * 6

    x2l_free = [("vV", 0), ("vV", 0)]
    d0_tp = [0, 0]
    d0_state = {}

    def d0_stats(t):
        k = t % 2
        sm = "dA" if k == 0 else "dB"
        ld = P.add("sync", lambda s, t=t, k=k: s.dma_start(
            out=x2l[k][:], in_=x2d_t[t]),
            wait=[("dX", x2_store[t]), x2l_free[k]], inc=sm, n=16)
        vop(lambda v, k=k, t=t: v.tensor_reduce(
            dstat[:, t:t + 1], x2l[k][:], axis=AX.X, op=ALU.add),
            wait=[(sm, ld)])
        vop(lambda v, t=t: v.tensor_scalar(
            dstat[:, 32 + t:33 + t], dstat[:, t:t + 1], -1.0 / D, None, ALU.mult))
        v_ctr = vop(lambda v, k=k, t=t: v.tensor_scalar(
            x2l[k][:], x2l[k][:], dstat[:, 32 + t:33 + t], None, ALU.add))
        s_sq = P.add("scalar", lambda s, k=k, t=t: s.activation(
            sqs[:], x2l[k][:], AF.Square, accum_out=dstat[:, 16 + t:17 + t]),
            wait=[("vV", v_ctr)], inc="sS")
        d0_state[t] = (k, s_sq)

    def d0_finish(t):
        k, s_sq = d0_state.pop(t)
        v_var = vop(lambda v, t=t: v.tensor_scalar(
            dstat[:, 16 + t:17 + t], dstat[:, 16 + t:17 + t], 1.0 / D, None,
            ALU.mult), wait=[("sS", s_sq)])
        s_std = P.add("scalar", lambda s, t=t: s.activation(
            dstat[:, t:t + 1], dstat[:, 16 + t:17 + t], AF.Sqrt, bias=cst[:, 0:1]),
            wait=[("vV", v_var)], inc="sS")
        v_rec = vop(lambda v, t=t: v.reciprocal(
            dstat[:, 48 + t:49 + t], dstat[:, t:t + 1]), wait=[("sS", s_std)])
        s_nrm = P.add("scalar", lambda s, k=k, t=t: s.activation(
            x2l[k][:], x2l[k][:], AF.Copy, scale=dstat[:, 48 + t:49 + t]),
            wait=[("vV", v_rec)], inc="sS")
        vop(lambda v, k=k, t=t: v.tensor_reduce(
            dstat[:, 64 + t:65 + t], x2l[k][:], axis=AX.X, op=ALU.max,
            apply_absolute_value=True), wait=[("sS", s_nrm)])
        vop(lambda v, t=t: v.tensor_scalar(
            small[:, 55:56], dstat[:, 64 + t:65 + t], 1e-5, None, ALU.max))
        vop(lambda v: v.reciprocal(small[:, 56:57], small[:, 55:56]))
        v_s57 = vop(lambda v: v.tensor_scalar(
            small[:, 57:58], small[:, 56:57], 127.0, None, ALU.mult))
        vop(lambda v, t=t: v.tensor_scalar(
            rsx[:, t:t + 1], small[:, 55:56], 1.0 / 127.0, None, ALU.mult))
        vop(lambda v, t=t: v.tensor_tensor(
            sgc[:, t:t + 1], rsx[:, t:t + 1], wscl[:, 3:4], ALU.mult),
            wait=[("vV", v_wscl)])
        for fi in range(2):
            b = 4 + fi
            s_q = P.add("scalar", lambda s, k=k, fi=fi: s.activation(
                wqi2[:], x2l[k][:, fi * 512:(fi + 1) * 512], AF.Copy,
                scale=small[:, 57:58]),
                wait=[("vV", v_s57), ("tP", d0_tp[fi])], inc="sS")
            if fi == 1:
                x2l_free[k] = ("sS", s_q)
            s_cast = P.add("scalar", lambda s, fi=fi: s.copy(
                xqb[:, fi * 512:(fi + 1) * 512], wqi2[:]), inc="sS")
            last = None
            for k4 in range(4):
                f = fi * 4 + k4
                w = [("sS", max(s_cast, pdbank_sS[b]))] if k4 == 0 else []
                last = P.add("tensor", lambda tt, f=f, k4=k4, b=b: tt.matmul(
                    pd[b][:, k4 * 128:(k4 + 1) * 128],
                    xqb[:, f * 128:(f + 1) * 128], ident[:],
                    start=(k4 == 0), stop=(k4 == 3), skip_group_check=True),
                    wait=w, inc=("tP" if k4 == 3 else None))
            d0_tp[fi] = last
            pdbank_sS[b] = P.add("scalar", lambda s, fi=fi, t=t, b=b: s.copy(
                xq_fm[:].rearrange("p (c q) -> p c q", q=TOWN)[
                    :, fi * 4:fi * 4 + 4, t * 128:t * 128 + 128],
                pd[b][:].rearrange("p (c q) -> p c q", q=128)),
                wait=[("tP", last)], inc="sS")
        xq_done[t] = pdbank_sS[5]

    xq_done = {}
    for idx in range(9):
        if idx >= 1:
            d0_finish(idx - 1)
        if idx < 8:
            d0_stats(idx)

    quant_rr = [0]
    wfs_free = [0, 0]

    def quant_tile(dst, dst_off, src_rr, ci, f0, mi, w_extra=()):
        k = quant_rr[0] % 2
        quant_rr[0] += 1
        sm = "dW0" if k == 0 else "dW1"
        w = list(w_extra) + [("vV", wfs_free[k])]
        ld = P.add("sync", lambda s, src_rr=src_rr, ci=ci, f0=f0, k=k: s.dma_start(
            out=wfs[k][:], in_=src_rr[ci][:, f0:f0 + 512]),
            wait=w, inc=sm, n=16)
        P.add("vector", lambda v, k=k, mi=mi: v.tensor_scalar(
            wfs[k][:], wfs[k][:], wscl[:, mi:mi + 1], CLIPW, ALU.mult, ALU.min),
            wait=[(sm, ld), ("vV", v_wscl)], inc="vV")
        wfs_free[k] = P.add("vector", lambda v, k=k: v.tensor_scalar(
            wqi[:], wfs[k][:], -CLIPW, None, ALU.max), inc="vV")
        return P.add("vector", lambda v, dst=dst, dst_off=dst_off: v.tensor_copy(
            dst[:, dst_off:dst_off + 512], wqi[:]), inc="vV")

    v_wdq = 0
    for ci in range(32):
        for fh in range(2):
            v_wdq = quant_tile(wdq, ci * 1024 + fh * 512, wd_r, ci, fh * 512, 2)

    hbr = hbuf[:].rearrange("p (t f) -> p t f", f=INNER)
    wgq_free = [0, 0]
    hqfm_free = [0]
    htile_vV = [0] * 8
    d0_nxt = [8]
    go_idx = [0]
    for H in range(2):
        for gu in range(2):
            mi = gu
            src_rr = wg_r if gu == 0 else wu_r
            for go in range(8):
                gb = go_idx[0] % 2
                go_idx[0] += 1
                if H == 0 and gu == 0 and d0_nxt[0] < 16:
                    t = d0_nxt[0]
                    d0_nxt[0] += 1
                    d0_stats(t)
                    d0_finish(t)
                v_wq = 0
                for ic in range(8):
                    v_wq = quant_tile(wgqs[gb], ic * 512, src_rr, ic, go * 512, mi,
                                      w_extra=([("tP", wgq_free[gb])] if ic == 0 else ()))
                for t8 in range(8):
                    tg = H * 8 + t8
                    b = t8 % 4
                    last = None
                    for ic in range(8):
                        w = []
                        if ic == 0:
                            w = [("vV", max(v_wq, pdbank_vV[b])),
                                 ("sS", max(pdbank_sS[b], xq_done[tg]))]
                        last = P.add("tensor", lambda tt, ic=ic, b=b, tg=tg, gb=gb: tt.matmul(
                            pd[b][:, 0:512],
                            xq_fm[:, ic * TOWN + tg * 128: ic * TOWN + tg * 128 + 128],
                            wgqs[gb][:, ic * 512: ic * 512 + 512],
                            start=(ic == 0), stop=(ic == 7), skip_group_check=True),
                            wait=w, inc=("tP" if ic == 7 else None))
                    if gu == 0:
                        sl = P.add("scalar", lambda s, b=b, t8=t8, go=go, tg=tg: s.activation(
                            hbr[:, t8, go * 512:go * 512 + 512],
                            pd[b][:], AF.Silu, scale=sgc[:, tg:tg + 1]),
                            wait=[("tP", last)], inc="sS")
                        pdbank_sS[b], pdbank_vV[b] = sl, 0
                        htile_vV[t8] = 0
                    else:
                        wv = [("tP", last)]
                        if go == 0 and t8 == 0:
                            wv.append(("sS", P.val("sS")))
                        v1 = P.add("vector", lambda v, b=b, t8=t8, go=go: v.tensor_tensor(
                            hbr[:, t8, go * 512:go * 512 + 512],
                            hbr[:, t8, go * 512:go * 512 + 512],
                            pd[b][:], ALU.mult), wait=wv, inc="vV")
                        pdbank_vV[b], pdbank_sS[b] = v1, 0
                        htile_vV[t8] = v1
                wgq_free[gb] = P.val("tP")
        for t8 in range(8):
            tg = H * 8 + t8
            k = t8 % 2
            vop(lambda v, t8=t8: v.tensor_reduce(
                small[:, 58:59], hbr[:, t8, :], axis=AX.X, op=ALU.max,
                apply_absolute_value=True),
                wait=[("vV", htile_vV[t8])])
            vop(lambda v: v.tensor_scalar(
                small[:, 59:60], small[:, 58:59], 1e-5, None, ALU.max))
            vop(lambda v: v.reciprocal(small[:, 60:61], small[:, 59:60]))
            vop(lambda v: v.tensor_scalar(
                small[:, 61:62], small[:, 60:61], 127.0, None, ALU.mult))
            vop(lambda v, tg=tg: v.tensor_tensor(
                small[:, 62:63], small[:, 59:60], rsx[:, tg:tg + 1], ALU.mult))
            v_hd = vop(lambda v: v.tensor_tensor(
                small[:, 62:63], small[:, 62:63], wscl[:, 6:7], ALU.mult))
            for icg in range(8):
                half = icg % 2
                b = 4 + half
                s_q = P.add("scalar", lambda s, t8=t8, icg=icg: s.activation(
                    wqi2[:], hbr[:, t8, icg * 512:(icg + 1) * 512], AF.Copy,
                    scale=small[:, 61:62]),
                    wait=[("vV", v_hd)], inc="sS")
                s_cast = P.add("scalar", lambda s, half=half: s.copy(
                    xqb[:, half * 512:(half + 1) * 512], wqi2[:]),
                    wait=[("tP", d0_tp[half])], inc="sS")
                last = None
                for k4 in range(4):
                    w = ([("sS", max(s_cast, pdbank_sS[b]))] if k4 == 0 else [])
                    last = P.add("tensor", lambda tt, k4=k4, b=b, half=half: tt.matmul(
                        pd[b][:, k4 * 128:(k4 + 1) * 128],
                        xqb[:, half * 512 + k4 * 128: half * 512 + (k4 + 1) * 128],
                        ident[:],
                        start=(k4 == 0), stop=(k4 == 3), skip_group_check=True),
                        wait=w, inc=("tP" if k4 == 3 else None))
                d0_tp[half] = last
                pdbank_sS[b] = P.add("scalar", lambda s, icg=icg, b=b, k=k: s.copy(
                    hq_fm[k][:, icg * 512:(icg + 1) * 512], pd[b][:]),
                    wait=[("tP", last), ("tP", hqfm_free[0])], inc="sS")
            evs = pdbank_sS[4 + ((8 - 1) % 2)]
            evs = max(pdbank_sS[4], pdbank_sS[5])
            smx = "dA" if k == 0 else "dB"
            ldx = P.add("sync", lambda s, tg=tg, k=k: s.dma_start(
                out=x2l[k][:], in_=x2d_t[tg]),
                wait=[x2l_free[k], ("dS", P.val("dS"))], inc=smx, n=16)
            for ho in range(2):
                b = ho
                last = None
                for ic in range(32):
                    w = []
                    if ic == 0:
                        w = [("sS", max(evs, pdbank_sS[b])),
                             ("vV", max(v_wdq, pdbank_vV[b]))]
                    last = P.add("tensor", lambda tt, ic=ic, b=b, ho=ho, k=k: tt.matmul(
                        pd[b][:, 0:512],
                        hq_fm[k][:, ic * 128:(ic + 1) * 128],
                        wdq[:, ic * 1024 + ho * 512: ic * 1024 + ho * 512 + 512],
                        start=(ic == 0), stop=(ic == 31), skip_group_check=True),
                        wait=w, inc=("tP" if ic == 31 else None))
                wv = [("tP", last), (smx, ldx)]
                ev = P.add("vector", lambda v, b=b, ho=ho, k=k: v.scalar_tensor_tensor(
                    x2l[k][:, ho * 512:(ho + 1) * 512], pd[b][:],
                    small[:, 62:63], x2l[k][:, ho * 512:(ho + 1) * 512],
                    ALU.mult, ALU.add),
                    wait=wv, inc="vV")
                pdbank_vV[b], pdbank_sS[b] = ev, 0
            hqfm_free[0] = P.val("tP")
            x2l_free[k] = ("vV", P.val("vV"))
            P.add("sync", lambda s, tg=tg, k=k: s.dma_start(
                out=yout_t[tg], in_=x2l[k][:]),
                wait=[("vV", P.val("vV"))], inc="dS", n=16)

    P.add("sync", None, wait=[("dS", P.val("dS")), ("dX", P.val("dX"))])
    P.add("gpsimd", None, wait=[("dS", P.val("dS"))])

    with nc.Block() as block:
        P.emit(block, sems)
    psD.close()
    dstk.close()
    top.close()
    return nc


_NC_CACHE = None


def kernel(L0_attn_hidden_states, L0_attn_Wq, L0_attn_Wk, L0_attn_Wv, L0_attn_Wo,
           L0_ffn_w_gate, L0_ffn_w_up, L0_ffn_w_down):
    global _NC_CACHE
    x = np.asarray(L0_attn_hidden_states, dtype=np.float32)
    wq = np.ascontiguousarray(np.asarray(L0_attn_Wq, np.float32).T)
    wk = np.ascontiguousarray(np.asarray(L0_attn_Wk, np.float32).T)
    wv = np.ascontiguousarray(np.asarray(L0_attn_Wv, np.float32).T)
    wo = np.ascontiguousarray(np.asarray(L0_attn_Wo, np.float32).T)
    wg = np.ascontiguousarray(np.asarray(L0_ffn_w_gate, np.float32).T)
    wu = np.ascontiguousarray(np.asarray(L0_ffn_w_up, np.float32).T)
    wd = np.ascontiguousarray(np.asarray(L0_ffn_w_down, np.float32).T)

    if _NC_CACHE is None:
        _NC_CACHE = build_nc()
    nc = _NC_CACHE

    in_maps = []
    perms = []
    for c in range(8):
        b, p = c // 2, c % 2
        tiles = list(range(p, 32, 2)) + list(range(1 - p, 32, 2))
        rows = np.concatenate([np.arange(t * 128, (t + 1) * 128) for t in tiles])
        perms.append(rows)
        xpm = np.ascontiguousarray(x[b][rows])
        mp = np.full((128, 128), float(p), np.float32)
        in_maps.append({"xp": xpm, "wqT": wq, "wkT": wk, "wvT": wv, "woT": wo,
                        "wgT": wg, "wuT": wu, "wdT": wd, "mpart": mp})

    res = run_bass_kernel_spmd(nc, in_maps, core_ids=list(range(8)))
    out = np.empty_like(x)
    for c in range(8):
        b = c // 2
        own = perms[c][:TOWN]
        out[b][own] = res.results[c]["yout"]
    return out


# version 14
# speedup vs baseline: 1.0613x; 1.0613x over previous
import numpy as np
import concourse.bass as bass
import concourse.mybir as mybir
from concourse.bass_utils import run_bass_kernel_spmd
from contextlib import ExitStack

F32 = mybir.dt.float32
BF16 = mybir.dt.bfloat16
I8 = mybir.dt.int8
AF = mybir.ActivationFunctionType
ALU = mybir.AluOpType
AX = mybir.AxisListType

D = 1024
T = 4096
TOWN = 2048
HD = 64
INNER = 4096
EPS_LN = 1e-5
EPS_AT = 1e-6
QSCALE = HD ** -0.5
CLIPW = 1.4999999

ENGINES = ("sync", "tensor", "vector", "scalar", "gpsimd")


class Prog:
    def __init__(self):
        self.ops = {e: [] for e in ENGINES}
        self.cnt = {}

    def newsem(self, name):
        self.cnt[name] = 0

    def add(self, eng, fn=None, wait=(), inc=None, n=1):
        out = None
        incs = ()
        if inc is not None:
            self.cnt[inc] += n
            out = self.cnt[inc]
            incs = ((inc, n),)
        wait = tuple((s, v) for (s, v) in wait if v and v > 0)
        self.ops[eng].append((fn, wait, incs))
        return out

    def val(self, sem):
        return self.cnt[sem]

    def emit(self, block, sems):
        def mk(eng_name):
            ops = self.ops[eng_name]

            def body(eng):
                for fn, waits, incs in ops:
                    for s, v in waits:
                        eng.wait_ge(sems[s], v)
                    instr = fn(eng) if fn is not None else None
                    for s, n in incs:
                        instr.then_inc(sems[s], n)
            return body
        block.sync(mk("sync"))
        block.tensor(mk("tensor"))
        block.vector(mk("vector"))
        block.scalar(mk("scalar"))
        block.gpsimd(mk("gpsimd"))
        self.ops = {e: [] for e in ENGINES}


def build_nc():
    nc = bass.Bass("TRN2", target_bir_lowering=False, debug=False)

    xp = nc.dram_tensor("xp", [T, D], F32, kind="ExternalInput").ap()
    wqT = nc.dram_tensor("wqT", [D, D], F32, kind="ExternalInput").ap()
    wkT = nc.dram_tensor("wkT", [D, D], F32, kind="ExternalInput").ap()
    wvT = nc.dram_tensor("wvT", [D, D], F32, kind="ExternalInput").ap()
    woT = nc.dram_tensor("woT", [D, D], F32, kind="ExternalInput").ap()
    wgTd = nc.dram_tensor("wgT", [D, INNER], F32, kind="ExternalInput").ap()
    wuTd = nc.dram_tensor("wuT", [D, INNER], F32, kind="ExternalInput").ap()
    wdTd = nc.dram_tensor("wdT", [INNER, D], F32, kind="ExternalInput").ap()
    mpin = nc.dram_tensor("mpart", [128, 128], F32, kind="ExternalInput").ap()
    yout = nc.dram_tensor("yout", [TOWN, D], F32, kind="ExternalOutput").ap()
    x2d = nc.dram_tensor("x2d", [TOWN, D], F32, kind="Internal")

    xp_t = xp.rearrange("(t p) d -> t p d", p=128)
    yout_t = yout.rearrange("(t p) d -> t p d", p=128)
    x2d_t = x2d.ap().rearrange("(t p) d -> t p d", p=128)
    wq_r = wqT.rearrange("(c p) f -> p c f", p=128)
    wk_r = wkT.rearrange("(c p) f -> p c f", p=128)
    wv_r = wvT.rearrange("(c p) f -> p c f", p=128)
    wo_r = woT.rearrange("(c p) f -> p c f", p=128)
    wg_r = wgTd.rearrange("(c p) f -> c p f", p=128)
    wu_r = wuTd.rearrange("(c p) f -> c p f", p=128)
    wd_r = wdTd.rearrange("(c p) f -> c p f", p=128)

    P = Prog()
    SEMS = ("dS", "dG", "tP", "vV", "sS", "dA", "dB", "dW0", "dW1",
            "dR0", "dR1", "dX")
    for s in SEMS:
        P.newsem(s)

    top = ExitStack()
    E = top.enter_context
    sems = {s: E(nc.semaphore(s)) for s in SEMS}

    ident = E(nc.sbuf_tensor("ident", [128, 128], BF16))
    mtri = E(nc.sbuf_tensor("mtri", [128, 128], BF16))
    mpart = E(nc.sbuf_tensor("mpart_sb", [128, 128], BF16))
    ones_r = E(nc.sbuf_tensor("ones_r", [1, 128], BF16))
    ones_c = E(nc.sbuf_tensor("ones_c", [128, 1], BF16))
    ones_cf = E(nc.sbuf_tensor("ones_cf", [128, 1], F32))
    ones_rf = E(nc.sbuf_tensor("ones_rf", [1, 128], F32))
    wstat = E(nc.sbuf_tensor("wstat", [128, 96], F32))
    wscl = E(nc.sbuf_tensor("wscl", [128, 8], F32))
    small = E(nc.sbuf_tensor("small", [128, 64], F32))
    rsx = E(nc.sbuf_tensor("rsx", [128, 16], F32))
    sgc = E(nc.sbuf_tensor("sgc", [128, 16], F32))
    cst = E(nc.sbuf_tensor("cst", [128, 2], F32))
    ones_sq = E(nc.sbuf_tensor("ones_sq", [128, 128], F32))

    abc = ExitStack()
    EB = abc.enter_context
    iof = EB(nc.sbuf_tensor("iof", [128, 128], BF16))
    iop = EB(nc.sbuf_tensor("iop", [128, 128], BF16))
    xn_fm = EB(nc.sbuf_tensor("xn_fm", [128, 8 * T], BF16))
    OF = EB(nc.sbuf_tensor("OF", [128, 8 * TOWN], BF16))
    K_sb = EB(nc.sbuf_tensor("K_sb", [128, 2 * T], BF16))
    Q_sb = EB(nc.sbuf_tensor("Q_sb", [128, 2 * TOWN], BF16))
    V1 = EB(nc.sbuf_tensor("V1", [128, 32 * 260], BF16))
    Pbuf = EB(nc.sbuf_tensor("Pbuf", [128, 4 * 512], BF16))
    wq_sl = EB(nc.sbuf_tensor("wq_sl", [128, 8 * 256], BF16))
    wk_sl = EB(nc.sbuf_tensor("wk_sl", [128, 8 * 256], BF16))
    wv_sl = EB(nc.sbuf_tensor("wv_sl", [128, 8 * 256], BF16))
    zrow = EB(nc.sbuf_tensor("zrow", [1, 2 * 512], BF16))
    vcs = EB(nc.sbuf_tensor("vcs", [1, 16 * 260], BF16))
    rz_sb = EB(nc.sbuf_tensor("rz_sb", [128, 512], F32))
    xld = [EB(nc.sbuf_tensor(f"xld{k}", [128, D], F32)) for k in range(2)]
    xnt = [EB(nc.sbuf_tensor(f"xnt{k}", [128, D], BF16)) for k in range(2)]
    st_s = EB(nc.sbuf_tensor("st_s", [128, 32], F32))
    st_m = EB(nc.sbuf_tensor("st_m", [128, 32], F32))
    st_q = EB(nc.sbuf_tensor("st_q", [128, 32], F32))
    st_r = EB(nc.sbuf_tensor("st_r", [128, 32], F32))
    wmb = [EB(nc.sbuf_tensor(f"wmb{k}", [128, 1024], F32)) for k in range(2)]
    xres = [EB(nc.sbuf_tensor(f"xres{k}", [128, D], F32)) for k in range(2)]
    x2t = [EB(nc.sbuf_tensor(f"x2t{k}", [128, D], F32)) for k in range(2)]

    psABC = ExitStack()
    EP = psABC.enter_context
    ps_s = [EP(nc.psum_tensor(f"ps_s{i}", [128, 512], F32)) for i in range(4)]
    ps_o = [EP(nc.psum_tensor(f"ps_o{i}", [65, 512], F32)) for i in range(2)]
    ps_v = EP(nc.psum_tensor("ps_v", [1, 260], F32))
    ps_y = EP(nc.psum_tensor("ps_y", [128, 512], F32))

    sbank_sS = [0, 0, 0, 0]
    sbank_vV = [0, 0, 0, 0]
    obank_free = [0, 0]

    P.add("gpsimd", lambda g: g.iota(iof[:], [[1, 128]], channel_multiplier=0,
                                     allow_small_or_imprecise_dtypes=True))
    P.add("gpsimd", lambda g: g.iota(iop[:], [[0, 128]], channel_multiplier=1,
                                     allow_small_or_imprecise_dtypes=True),
          inc="dG", n=16)
    v_io = P.val("dG")
    v_mp = P.add("gpsimd", lambda g: g.dma_start(out=mpart[:], in_=mpin), inc="dG", n=16)
    v_cst = P.add("vector", lambda v: v.memset(cst[:, 0:1], EPS_LN), inc="vV")
    P.add("vector", lambda v: v.memset(ones_sq[:], 1.0), inc="vV")
    P.add("vector", lambda v: v.memset(ones_r[:], 1.0), inc="vV")
    P.add("vector", lambda v: v.memset(ones_c[:], 1.0), inc="vV")
    P.add("vector", lambda v: v.memset(ones_cf[:], 1.0), inc="vV")
    P.add("vector", lambda v: v.memset(ones_rf[:], 1.0), inc="vV")
    P.add("vector", lambda v: v.tensor_tensor(ident[:], iof[:], iop[:], ALU.is_equal),
          wait=[("dG", v_io)], inc="vV")
    v_const_all = P.add("vector", lambda v: v.tensor_tensor(mtri[:], iop[:], iof[:], ALU.is_le),
                        inc="vV")

    xn_ready = [0] * 2
    xn_freed = 0
    xld_freed = [0, 0]
    stats_state = {}
    vchain = [0]

    def vop(fn, wait=()):
        w = list(wait) + [("vV", vchain[0])]
        vchain[0] = P.add("vector", fn, wait=w, inc="vV")
        return vchain[0]


    def emit_stats(t):
        kx = t % 2
        k4 = t % 2
        sm = "dA" if t % 2 == 0 else "dB"
        ld = P.add("sync", lambda s, kx=kx, t=t: s.dma_start(
            out=xld[kx][:], in_=xp_t[t]),
            wait=[("vV", xld_freed[kx])], inc=sm, n=16)
        v_red = vop(lambda v, kx=kx, t=t: v.tensor_reduce(
            st_s[:, t:t + 1], xld[kx][:], axis=AX.X, op=ALU.add),
            wait=[(sm, ld)])
        vop(lambda v, t=t: v.tensor_scalar(
            st_m[:, t:t + 1], st_s[:, t:t + 1], -1.0 / D, None, ALU.mult))
        vop(lambda v, kx=kx, k4=k4, t=t: v.tensor_scalar(
            xnt[k4][:], xld[kx][:], st_m[:, t:t + 1], None, ALU.add),
            wait=[("tP", xn_freed)])
        xld_freed[kx] = vop(lambda v, k4=k4: v.tensor_tensor(
            wmb[0][:], xnt[k4][:], xnt[k4][:], ALU.mult))
        vop(lambda v, t=t: v.tensor_reduce(
            st_q[:, t:t + 1], wmb[0][:], axis=AX.X, op=ALU.add))
        v_var = vop(lambda v, t=t: v.tensor_scalar(
            st_q[:, t:t + 1], st_q[:, t:t + 1], 1.0 / D, None, ALU.mult))
        stats_state[t] = (kx, v_var, 0)

    def emit_finish(t):
        kx, v_var, _ = stats_state.pop(t)
        k4 = t % 2
        s_std = P.add("scalar", lambda s, t=t: s.activation(
            st_s[:, t:t + 1], st_q[:, t:t + 1], AF.Sqrt, bias=cst[:, 0:1]),
            wait=[("vV", max(v_var, v_cst))], inc="sS")
        vop(lambda v, t=t: v.reciprocal(
            st_r[:, t:t + 1], st_s[:, t:t + 1]),
            wait=[("sS", s_std)])
        xn_ready[k4] = vop(lambda v, k4=k4, t=t: v.tensor_scalar(
            xnt[k4][:], xnt[k4][:], st_r[:, t:t + 1], None, ALU.mult))

    def emit_transposes(tg):
        for f in range(8):
            b = f % 2
            last = None
            for k4 in range(2):
                w = [("vV", max(xn_ready + [v_const_all])), ("sS", sbank_sS[b])] if k4 == 0 else []
                last = P.add("tensor", lambda tt, f=f, k4=k4, b=b: tt.matmul(
                    ps_s[b][:, k4 * 128:(k4 + 1) * 128],
                    xnt[k4][:, f * 128:(f + 1) * 128], ident[:],
                    start=(k4 == 0), stop=(k4 == 1), skip_group_check=True),
                    wait=w, inc=("tP" if k4 == 1 else None))
            sbank_sS[b] = P.add("scalar", lambda s, f=f, tg=tg, b=b: s.copy(
                xn_fm[:, f * T + tg * 256: f * T + tg * 256 + 256],
                ps_s[b][:, 0:256]),
                wait=[("tP", last)], inc="sS")
        return P.val("tP")

    for idx in range(33):
        if idx >= 1:
            t = idx - 1
            emit_finish(t)
            if t % 2 == 1:
                xn_freed = emit_transposes(t // 2)
        if idx < 32:
            emit_stats(idx)
    xnfm_all = P.val("sS")

    V1r = V1[:].rearrange("p (s c) -> p s c", c=260)
    v_v1ones = None
    for h in range(4):
        v_v1ones = P.add("vector", lambda v, h=h: v.memset(
            V1r[:, :, 65 * h + 64:65 * h + 65], 1.0), inc="vV")

    wmean_tiles = []
    for (rr, nch) in ((wg_r, 8), (wu_r, 8), (wd_r, 32)):
        for ci in range(nch):
            nb = 4 if rr is not wd_r else 1
            for fb in range(nb):
                wmean_tiles.append((rr, ci, fb * 1024))
    assert len(wmean_tiles) == 96
    wm_abs = {}

    def emit_wmean():
        idx = len(wm_abs)
        if idx >= 96:
            return
        rr, ci, f0 = wmean_tiles[idx]
        buf = wmb[idx % 2]
        sm = "dW0" if idx % 2 == 0 else "dW1"
        w = [("vV", wm_abs[idx - 2])] if idx >= 2 else [("sS", xnfm_all)]
        ld = P.add("sync", lambda s, rr=rr, ci=ci, f0=f0, buf=buf: s.dma_start(
            out=buf[:], in_=rr[ci][:, f0:f0 + 1024]), wait=w, inc=sm, n=16)
        wm_abs[idx] = P.add("vector", lambda v, buf=buf, idx=idx: v.tensor_reduce(
            wstat[:, idx:idx + 1], buf[:], axis=AX.X, op=ALU.add,
            apply_absolute_value=True),
            wait=[(sm, ld)], inc="vV")

    bank_rr = [0]
    prev_av_last = [0]
    pending_norm = []
    norm_done = {}
    zslot_free = [0]
    zb_free = [0]

    def emit_norm():
        if not pending_norm:
            return
        cp, J, i2, e4 = pending_norm.pop(0)
        P.add("tensor", lambda tt, i2=i2: tt.matmul(
            ps_y[0:64, :], ones_r[0:1, 0:64],
            zrow[0:1, (2 * i2) * 512:(2 * i2) * 512 + 512],
            start=True, stop=False, skip_group_check=True),
            wait=[("sS", e4), ("vV", zb_free[0])])
        mmz = P.add("tensor", lambda tt, i2=i2: tt.matmul(
            ps_y[64:128, :], ones_r[0:1, 0:64],
            zrow[0:1, (2 * i2 + 1) * 512:(2 * i2 + 1) * 512 + 512],
            start=True, stop=True, skip_group_check=True), inc="tP")
        zslot_free[i2] = mmz
        zb_free[0] = P.add("vector", lambda v: v.tensor_scalar(
            rz_sb[:], ps_y[:], 2 * EPS_AT, None, ALU.add),
            wait=[("tP", mmz)], inc="vV")
        P.add("vector", lambda v: v.reciprocal(rz_sb[:], rz_sb[:]))
        norm_done[(cp, J)] = P.add("vector", lambda v, cp=cp, J=J: v.tensor_tensor(
            OF[:, cp * TOWN + J * 512: cp * TOWN + J * 512 + 512],
            OF[:, cp * TOWN + J * 512: cp * TOWN + J * 512 + 512],
            rz_sb[:], ALU.mult), inc="vV")

    v_mp_safe = None
    for g in range(4):
        wsl_ld = None
        w0 = [("tP", P.val("tP"))] if g > 0 else []
        for (dst, src) in ((wq_sl, wq_r), (wk_sl, wk_r), (wv_sl, wv_r)):
            wsl_ld = P.add("gpsimd", lambda gg, dst=dst, src=src, g=g: gg.dma_start(
                out=dst[:].rearrange("p (c f) -> p c f", f=256),
                in_=src[:, :, g * 256:(g + 1) * 256]),
                wait=w0, inc="dG", n=16)
            w0 = []
        if g == 0:
            v_mp_safe = wsl_ld

        def proj_tile(lhs_sl, wcol, rhs_off, width, first_w, ev_fn, ev_eng):
            b = bank_rr[0] % 4
            bank_rr[0] += 1
            last = None
            for ic in range(8):
                w = []
                if ic == 0:
                    w = [("sS", sbank_sS[b]), ("vV", sbank_vV[b])] + first_w
                last = P.add("tensor", lambda tt, ic=ic, b=b, lhs_sl=lhs_sl, wcol=wcol,
                             rhs_off=rhs_off, width=width: tt.matmul(
                    ps_s[b][:, 0:width],
                    lhs_sl[:, ic * 256 + wcol: ic * 256 + wcol + 128],
                    xn_fm[:, ic * T + rhs_off: ic * T + rhs_off + width],
                    start=(ic == 0), stop=(ic == 7), skip_group_check=True),
                    wait=w, inc=("tP" if ic == 7 else None))
            ev = P.add(ev_eng, lambda e, b=b: ev_fn(e, ps_s[b]),
                       wait=[("tP", last)],
                       inc=("sS" if ev_eng == "scalar" else "vV"))
            if ev_eng == "scalar":
                sbank_sS[b], sbank_vV[b] = ev, 0
            else:
                sbank_sS[b], sbank_vV[b] = 0, ev

        ei = 0
        for kc in range(2):
            for t8 in range(8):
                fw = [("dG", wsl_ld), ("sS", xnfm_all)] if ei == 0 else []
                dsto = kc * T + t8 * 512
                if ei % 2 == 0:
                    proj_tile(wk_sl, kc * 128, t8 * 512, 512, fw,
                              lambda e, psb, dsto=dsto: e.copy(
                                  K_sb[:, dsto:dsto + 512], psb[:, 0:512]), "scalar")
                else:
                    proj_tile(wk_sl, kc * 128, t8 * 512, 512, fw,
                              lambda e, psb, dsto=dsto: e.tensor_copy(
                                  K_sb[:, dsto:dsto + 512], psb[:, 0:512]), "vector")
                ei += 1
        for kc in range(2):
            for t4 in range(4):
                dsto = kc * TOWN + t4 * 512
                proj_tile(wq_sl, kc * 128, t4 * 512, 512, [],
                          lambda e, psb, dsto=dsto: e.tensor_scalar(
                              Q_sb[:, dsto:dsto + 512], psb[:, 0:512],
                              QSCALE, None, ALU.mult), "vector")
                ei += 1
        for t1 in range(32):
            b = bank_rr[0] % 4
            bank_rr[0] += 1
            last = None
            for ic in range(8):
                w = [("sS", sbank_sS[b]), ("vV", sbank_vV[b])] if ic == 0 else []
                last = P.add("tensor", lambda tt, ic=ic, b=b, t1=t1: tt.matmul(
                    ps_s[b][:, 0:256],
                    xn_fm[:, ic * T + t1 * 128: ic * T + t1 * 128 + 128],
                    wv_sl[:, ic * 256: ic * 256 + 256],
                    start=(ic == 0), stop=(ic == 7), skip_group_check=True),
                    wait=w, inc=("tP" if ic == 7 else None))
            evv = None
            for h in range(4):
                w = [("tP", last)] if h == 0 else []
                if g == 0 and t1 == 0 and h == 0:
                    w.append(("vV", v_v1ones))
                evv = P.add("vector", lambda v, b=b, t1=t1, h=h: v.tensor_copy(
                    V1[:, t1 * 260 + 65 * h: t1 * 260 + 65 * h + 64],
                    ps_s[b][:, h * 64:(h + 1) * 64]),
                    wait=w, inc="vV")
            sbank_sS[b], sbank_vV[b] = 0, evv
            if t1 % 2 == 1:
                emit_wmean()
        v_proj_sS = P.val("sS")
        v_proj_vV = P.val("vV")

        last_copy = 0
        for t in range(1, 16):
            w = ([("vV", v_proj_vV), ("sS", v_proj_sS)] if t == 1
                 else [("sS", last_copy)])
            P.add("tensor", lambda tt, t=t: tt.matmul(
                ps_v[:], ones_c[:], V1[:, (t - 1) * 260:(t - 1) * 260 + 260],
                start=(t == 1), stop=False, skip_group_check=True), wait=w)
            mm2 = P.add("tensor", lambda tt, t=t: tt.matmul(
                ps_v[:], ones_c[:], V1[:, (16 + t - 1) * 260:(16 + t - 1) * 260 + 260],
                start=False, stop=(t == 15), skip_group_check=True), inc="tP")
            last_copy = P.add("scalar", lambda s, t=t: s.copy(
                vcs[0:1, t * 260:(t + 1) * 260], ps_v[:]),
                wait=[("tP", mm2)], inc="sS")
        v_snap = last_copy

        for lp in range(2):
            cp = 2 * g + lp
            for J in range(4):
                emit_wmean()
                slots = list(range(4 * J + 4)) + [16 + r for r in range(4 * J + 4)]
                ns = len(slots)
                pA, pB, sttA, sttB, avB, smm = {}, {}, {}, {}, {}, {}

                def c0_of(sl):
                    return 128 * max(0, (sl % 16) - 4 * J)

                def emit_av(si, lp=lp, J=J, slots=slots, sttA=sttA, sttB=sttB,
                            pA=pA, pB=pB, avB=avB):
                    sl = slots[si]
                    c0 = 128 * max(0, (sl % 16) - 4 * J)
                    bb = si % 2
                    boundary = (sl % 16) >= 4 * J
                    wA = [("vV", sttB[si])] if boundary else [("sS", pA[si])]
                    if si == 0:
                        wA += [("sS", obank_free[0]), ("sS", obank_free[1])]
                    P.add("tensor", lambda tt, sl=sl, c0=c0, bb=bb, si=si, lp=lp: tt.matmul(
                        ps_o[0][:, c0:512],
                        V1[:, sl * 260 + 65 * (2 * lp): sl * 260 + 65 * (2 * lp) + 65],
                        Pbuf[:, (2 * bb) * 512 + c0:(2 * bb) * 512 + 512],
                        start=(si == 0), stop=False, skip_group_check=True),
                        wait=wA)
                    wB = [("vV", sttB[si])] if boundary else [("sS", pB[si])]
                    avB[si] = P.add("tensor", lambda tt, sl=sl, c0=c0, bb=bb, si=si, lp=lp: tt.matmul(
                        ps_o[1][:, c0:512],
                        V1[:, sl * 260 + 65 * (2 * lp + 1): sl * 260 + 65 * (2 * lp + 1) + 65],
                        Pbuf[:, (2 * bb + 1) * 512 + c0:(2 * bb + 1) * 512 + 512],
                        start=(si == 0), stop=False, skip_group_check=True),
                        wait=wB, inc="tP")

                for si, sl in enumerate(slots):
                    c0 = c0_of(sl)
                    bb = si % 2
                    bA, bB = 2 * bb, 2 * bb + 1
                    boundary = (sl % 16) >= 4 * J
                    if si < 2:
                        w = [("sS", max(sbank_sS[bA], sbank_sS[bB], v_snap)),
                             ("vV", max(sbank_vV[bA], sbank_vV[bB]))]
                        sbank_sS[bA] = sbank_sS[bB] = 0
                        sbank_vV[bA] = sbank_vV[bB] = 0
                    else:
                        w = [("sS", pB[si - 2])]
                    kcol = lp * T + sl * 128
                    qcol = lp * TOWN + J * 512 + c0
                    P.add("tensor", lambda tt, bA=bA, c0=c0, kcol=kcol, qcol=qcol: tt.matmul(
                        ps_s[bA][:, c0:512], K_sb[0:64, kcol:kcol + 128],
                        Q_sb[0:64, qcol:qcol + 512 - c0],
                        start=True, stop=True, tile_position=(0, 0),
                        skip_group_check=True), wait=w)
                    smm[si] = P.add("tensor", lambda tt, bB=bB, c0=c0, kcol=kcol, qcol=qcol: tt.matmul(
                        ps_s[bB][:, c0:512], K_sb[64:128, kcol:kcol + 128],
                        Q_sb[64:128, qcol:qcol + 512 - c0],
                        start=True, stop=True, tile_position=(64, 0),
                        skip_group_check=True), inc="tP")
                    if si == 2:
                        emit_norm()
                    if si >= 1:
                        emit_av(si - 1)
                    pw = [("tP", smm[si])]
                    if si >= 2:
                        pw.append(("tP", avB[si - 2]))
                    elif prev_av_last[0]:
                        pw.append(("tP", prev_av_last[0]))
                    pA[si] = P.add("scalar", lambda s, bb=bb, bA=bA, c0=c0: s.activation(
                        Pbuf[:, (2 * bb) * 512 + c0:(2 * bb) * 512 + 512],
                        ps_s[bA][:, c0:512], AF.Square, bias=1.0),
                        wait=pw, inc="sS")
                    pB[si] = P.add("scalar", lambda s, bb=bb, bB=bB, c0=c0: s.activation(
                        Pbuf[:, (2 * bb + 1) * 512 + c0:(2 * bb + 1) * 512 + 512],
                        ps_s[bB][:, c0:512], AF.Square, bias=1.0),
                        inc="sS")
                    if boundary:
                        mk_ = mtri if sl < 16 else mpart
                        mw = [("sS", pA[si])]
                        if sl >= 16:
                            mw.append(("dG", v_mp_safe))
                        sttA[si] = P.add("vector", lambda v, bb=bb, c0=c0, mk_=mk_: v.scalar_tensor_tensor(
                            Pbuf[:, (2 * bb) * 512 + c0:(2 * bb) * 512 + c0 + 128],
                            Pbuf[:, (2 * bb) * 512 + c0:(2 * bb) * 512 + c0 + 128],
                            1.0, mk_[:], ALU.add, ALU.mult), wait=mw, inc="vV")
                        sttB[si] = P.add("vector", lambda v, bb=bb, c0=c0, mk_=mk_: v.scalar_tensor_tensor(
                            Pbuf[:, (2 * bb + 1) * 512 + c0:(2 * bb + 1) * 512 + c0 + 128],
                            Pbuf[:, (2 * bb + 1) * 512 + c0:(2 * bb + 1) * 512 + c0 + 128],
                            1.0, mk_[:], ALU.add, ALU.mult),
                            wait=[("sS", pB[si])], inc="vV")
                emit_av(ns - 1)
                sbank_sS[0] = sbank_sS[1] = pB[ns - 2]
                sbank_sS[2] = sbank_sS[3] = pB[ns - 1]
                prev_av_last[0] = avB[ns - 1]
                lastk = avB[ns - 1]
                for i in range(4):
                    t = 4 * J + i
                    if t == 0:
                        continue
                    for oi, hl in ((0, 2 * lp), (1, 2 * lp + 1)):
                        stop = (i == 3) and (oi == 1)
                        lastk = P.add("tensor", lambda tt, t=t, oi=oi, hl=hl, i=i, stop=stop: tt.matmul(
                            ps_o[oi][:, i * 128:(i + 1) * 128],
                            vcs[0:1, t * 260 + 65 * hl: t * 260 + 65 * hl + 65],
                            ones_r[:, 0:128],
                            start=False, stop=stop, skip_group_check=True),
                            inc=("tP" if stop else None))
                i2 = 0
                P.add("scalar", lambda s, cp=cp, J=J: s.copy(
                    OF[0:64, cp * TOWN + J * 512: cp * TOWN + J * 512 + 512],
                    ps_o[0][0:64, :]), wait=[("tP", lastk)], inc="sS")
                P.add("scalar", lambda s, cp=cp, J=J: s.copy(
                    OF[64:128, cp * TOWN + J * 512: cp * TOWN + J * 512 + 512],
                    ps_o[1][0:64, :]), inc="sS")
                e3 = P.add("scalar", lambda s, i2=i2: s.copy(
                    zrow[0:1, (2 * i2) * 512:(2 * i2) * 512 + 512],
                    ps_o[0][64:65, :]),
                    wait=[("tP", zslot_free[i2])], inc="sS")
                e4 = P.add("scalar", lambda s, i2=i2: s.copy(
                    zrow[0:1, (2 * i2 + 1) * 512:(2 * i2 + 1) * 512 + 512],
                    ps_o[1][64:65, :]), inc="sS")
                obank_free[0], obank_free[1] = e3, e4
                pending_norm.append((cp, J, i2, e4))
    while pending_norm:
        emit_norm()
    while len(wm_abs) < 96:
        emit_wmean()
    v_wmean_all = wm_abs[95]

    v_wo = P.add("gpsimd", lambda gg: gg.dma_start(
        out=K_sb[:].rearrange("p (c f) -> p c f", f=1024), in_=wo_r[:, :, :]),
        wait=[("tP", P.val("tP"))], inc="dG", n=16)

    for mi, (a, bcnt) in enumerate(((0, 32), (32, 32), (64, 32))):
        vop(lambda v, mi=mi, a=a, bcnt=bcnt: v.tensor_reduce(
            small[:, 32 + mi:33 + mi], wstat[:, a:a + bcnt], axis=AX.X, op=ALU.add),
            wait=[("vV", v_wmean_all)])
    vop(lambda v: v.tensor_copy(small[:, 36:39], small[:, 32:35]))
    for _ in range(4):
        vop(lambda v: v.tensor_copy(small[:, 0:1], small[:, 36:37]))
    v_cast = vop(lambda v: v.tensor_copy(small[:, 0:1], small[:, 36:37]))
    mmw = P.add("tensor", lambda tt: tt.matmul(
        ps_y[:, 0:3], ones_sq[:], small[:, 36:39],
        start=True, stop=True, skip_group_check=True),
        wait=[("vV", v_cast)], inc="tP")
    nels = [float(D * INNER), float(D * INNER), float(INNER * D)]
    for mi in range(3):
        vop(lambda v, mi=mi: v.tensor_scalar(
            wscl[:, 3 + mi:4 + mi], ps_y[:, mi:mi + 1], 1.0 / nels[mi],
            None, ALU.mult),
            wait=([("tP", mmw)] if mi == 0 else []))
        vop(lambda v, mi=mi: v.tensor_scalar(
            wscl[:, 3 + mi:4 + mi], wscl[:, 3 + mi:4 + mi], 1e-5, None, ALU.max))
        vop(lambda v, mi=mi: v.reciprocal(wscl[:, mi:mi + 1], wscl[:, 3 + mi:4 + mi]))
    vop(lambda v: v.tensor_tensor(
        wscl[:, 6:7], wscl[:, 4:5], wscl[:, 5:6], ALU.mult))
    v_wscl = vop(lambda v: v.tensor_scalar(
        wscl[:, 6:7], wscl[:, 6:7], 1.0 / 127.0, None, ALU.mult))

    xres_free = [0, 0]
    x2_store = {}
    for tt16 in range(16):
        k = tt16 % 2
        smR = "dR0" if k == 0 else "dR1"
        ld = P.add("sync", lambda s, tt16=tt16, k=k: s.dma_start(
            out=xres[k][:], in_=xp_t[tt16]),
            wait=[("vV", xres_free[k])], inc=smR, n=16)
        evs = []
        for half in range(2):
            b = bank_rr[0] % 4
            bank_rr[0] += 1
            last = None
            for oc in range(8):
                w = []
                if oc == 0:
                    w = [("sS", sbank_sS[b]), ("vV", sbank_vV[b]),
                         ("vV", max(norm_done[(cc, tt16 // 4)] for cc in range(8)))]
                    if tt16 == 0 and half == 0:
                        w.append(("dG", v_wo))
                last = P.add("tensor", lambda tt, oc=oc, b=b, tt16=tt16, half=half: tt.matmul(
                    ps_s[b][:, 0:512],
                    OF[:, oc * TOWN + tt16 * 128: oc * TOWN + tt16 * 128 + 128],
                    K_sb[:, oc * 1024 + half * 512: oc * 1024 + half * 512 + 512],
                    start=(oc == 0), stop=(oc == 7), skip_group_check=True),
                    wait=w, inc=("tP" if oc == 7 else None))
            wv = [("tP", last), (smR, ld)]
            if tt16 >= 2:
                wv.append(("dX", x2_store[tt16 - 1]))
            ev = P.add("vector", lambda v, b=b, half=half, k=k: v.tensor_tensor(
                x2t[k][:, half * 512:(half + 1) * 512], ps_s[b][:, 0:512],
                xres[k][:, half * 512:(half + 1) * 512], ALU.add),
                wait=wv, inc="vV")
            evs.append(ev)
            sbank_sS[b], sbank_vV[b] = 0, ev
        xres_free[k] = evs[-1]
        x2_store[tt16] = P.add("sync", lambda s, tt16=tt16, k=k: s.dma_start(
            out=x2d_t[tt16], in_=x2t[k][:]),
            wait=[("vV", evs[-1])], inc="dX", n=16)

    with nc.Block() as block:
        P.emit(block, sems)
    psABC.close()
    abc.close()

    dstk = ExitStack()
    ED = dstk.enter_context
    xq_fm = ED(nc.sbuf_tensor("xq_fm", [128, 8 * TOWN], BF16))
    hbuf = ED(nc.sbuf_tensor("hbuf", [128, 8 * INNER], BF16))
    wdq = ED(nc.sbuf_tensor("wdq", [128, 32 * 1024], BF16))
    wgqs = [ED(nc.sbuf_tensor(f"wgq{k}", [128, 8 * 512], BF16)) for k in range(2)]
    wfs = [ED(nc.sbuf_tensor(f"wfs{k}", [128, 512], F32)) for k in range(2)]
    wqa = ED(nc.sbuf_tensor("wqa", [128, 512], F32))
    wqi = ED(nc.sbuf_tensor("wqi", [128, 512], I8))
    x2l = [ED(nc.sbuf_tensor(f"x2l{k}", [128, D], F32)) for k in range(2)]
    sqs = ED(nc.sbuf_tensor("sqs", [128, D], F32))
    xqb = ED(nc.sbuf_tensor("xqb", [128, D], BF16))
    hq_fm = [ED(nc.sbuf_tensor("hqfm0", [128, INNER], BF16))] * 2

    psD = ExitStack()
    EPD = psD.enter_context
    pd = [EPD(nc.psum_tensor(f"pd{i}", [128, 512], F32)) for i in range(6)]
    pd_w = EPD(nc.psum_tensor("pd_w", [1, 4], F32))
    pd_b = EPD(nc.psum_tensor("pd_b", [128, 1], F32))
    dstat = wstat
    pdbank_sS = [0] * 6
    pdbank_vV = [0] * 6

    x2l_free = [0, 0]
    d0_state = {}

    def d0_stats(t):
        k = t % 2
        sm = "dA" if k == 0 else "dB"
        ld = P.add("sync", lambda s, t=t, k=k: s.dma_start(
            out=x2l[k][:], in_=x2d_t[t]),
            wait=[("dX", x2_store[t]), ("vV", x2l_free[k])], inc=sm, n=16)
        vop(lambda v, k=k, t=t: v.tensor_reduce(
            dstat[:, t:t + 1], x2l[k][:], axis=AX.X, op=ALU.add),
            wait=[(sm, ld)])
        vop(lambda v, t=t: v.tensor_scalar(
            dstat[:, 32 + t:33 + t], dstat[:, t:t + 1], -1.0 / D, None, ALU.mult))
        vop(lambda v, k=k, t=t: v.tensor_scalar(
            x2l[k][:], x2l[k][:], dstat[:, 32 + t:33 + t], None, ALU.add))
        vop(lambda v, k=k: v.tensor_tensor(
            sqs[:], x2l[k][:], x2l[k][:], ALU.mult))
        vop(lambda v, t=t: v.tensor_reduce(
            dstat[:, 16 + t:17 + t], sqs[:], axis=AX.X, op=ALU.add))
        v_var = vop(lambda v, t=t: v.tensor_scalar(
            dstat[:, 16 + t:17 + t], dstat[:, 16 + t:17 + t], 1.0 / D, None,
            ALU.mult))
        d0_state[t] = (k, v_var)

    def d0_finish(t):
        k, v_var = d0_state.pop(t)
        s_std = P.add("scalar", lambda s, t=t: s.activation(
            dstat[:, t:t + 1], dstat[:, 16 + t:17 + t], AF.Sqrt, bias=cst[:, 0:1]),
            wait=[("vV", v_var)], inc="sS")
        vop(lambda v, t=t: v.reciprocal(
            dstat[:, 48 + t:49 + t], dstat[:, t:t + 1]), wait=[("sS", s_std)])
        vop(lambda v, k=k, t=t: v.tensor_scalar(
            x2l[k][:], x2l[k][:], dstat[:, 48 + t:49 + t], None, ALU.mult))
        vop(lambda v, k=k, t=t: v.tensor_reduce(
            dstat[:, 64 + t:65 + t], x2l[k][:], axis=AX.X, op=ALU.max,
            apply_absolute_value=True))
        vop(lambda v, t=t: v.tensor_scalar(
            small[:, 55:56], dstat[:, 64 + t:65 + t], 1e-5, None, ALU.max))
        vop(lambda v: v.reciprocal(small[:, 56:57], small[:, 55:56]))
        vop(lambda v: v.tensor_scalar(
            small[:, 57:58], small[:, 56:57], 127.0, None, ALU.mult))
        vop(lambda v, t=t: v.tensor_scalar(
            rsx[:, t:t + 1], small[:, 55:56], 1.0 / 127.0, None, ALU.mult))
        vop(lambda v, t=t: v.tensor_tensor(
            sgc[:, t:t + 1], rsx[:, t:t + 1], wscl[:, 3:4], ALU.mult),
            wait=[("vV", v_wscl)])
        for fi in range(2):
            b = 4 + fi
            vm = vop(lambda v, k=k, fi=fi: v.tensor_scalar(
                wqi[:], x2l[k][:, fi * 512:(fi + 1) * 512], small[:, 57:58],
                None, ALU.mult))
            if fi == 1:
                x2l_free[k] = vm
            v_xq = vop(lambda v, fi=fi: v.tensor_copy(
                xqb[:, fi * 512:(fi + 1) * 512], wqi[:]))
            last = None
            for k4 in range(4):
                f = fi * 4 + k4
                w = [("vV", v_xq), ("sS", pdbank_sS[b])] if k4 == 0 else []
                last = P.add("tensor", lambda tt, f=f, k4=k4, b=b: tt.matmul(
                    pd[b][:, k4 * 128:(k4 + 1) * 128],
                    xqb[:, f * 128:(f + 1) * 128], ident[:],
                    start=(k4 == 0), stop=(k4 == 3), skip_group_check=True),
                    wait=w, inc=("tP" if k4 == 3 else None))
            pdbank_sS[b] = P.add("scalar", lambda s, fi=fi, t=t, b=b: s.copy(
                xq_fm[:].rearrange("p (c q) -> p c q", q=TOWN)[
                    :, fi * 4:fi * 4 + 4, t * 128:t * 128 + 128],
                pd[b][:].rearrange("p (c q) -> p c q", q=128)),
                wait=[("tP", last)], inc="sS")
        xq_done[t] = pdbank_sS[5]

    xq_done = {}
    for idx in range(9):
        if idx >= 1:
            d0_finish(idx - 1)
        if idx < 8:
            d0_stats(idx)

    quant_rr = [0]
    wfs_free = [0, 0]

    def quant_tile(dst, dst_off, src_rr, ci, f0, mi, w_extra=()):
        k = quant_rr[0] % 2
        quant_rr[0] += 1
        sm = "dW0" if k == 0 else "dW1"
        w = list(w_extra) + [("vV", wfs_free[k])]
        ld = P.add("sync", lambda s, src_rr=src_rr, ci=ci, f0=f0, k=k: s.dma_start(
            out=wfs[k][:], in_=src_rr[ci][:, f0:f0 + 512]),
            wait=w, inc=sm, n=16)
        v1 = P.add("vector", lambda v, k=k, mi=mi: v.tensor_scalar(
            wqa[:], wfs[k][:], wscl[:, mi:mi + 1], CLIPW, ALU.mult, ALU.min),
            wait=[(sm, ld), ("vV", v_wscl)], inc="vV")
        wfs_free[k] = v1
        P.add("vector", lambda v: v.tensor_scalar(
            wqi[:], wqa[:], -CLIPW, None, ALU.max))
        return P.add("vector", lambda v, dst=dst, dst_off=dst_off: v.tensor_copy(
            dst[:, dst_off:dst_off + 512], wqi[:]), inc="vV")

    v_wdq = 0
    for ci in range(32):
        for fh in range(2):
            v_wdq = quant_tile(wdq, ci * 1024 + fh * 512, wd_r, ci, fh * 512, 2)

    hbr = hbuf[:].rearrange("p (t f) -> p t f", f=INNER)
    wgq_free = [0, 0]
    hqfm_free = [0]
    htile_vV = [0] * 8
    d0_nxt = [8]
    go_idx = [0]
    for H in range(2):
        for gu in range(2):
            mi = gu
            src_rr = wg_r if gu == 0 else wu_r
            for go in range(8):
                gb = go_idx[0] % 2
                go_idx[0] += 1
                if H == 0 and gu == 0 and d0_nxt[0] < 16:
                    t = d0_nxt[0]
                    d0_nxt[0] += 1
                    d0_stats(t)
                    d0_finish(t)
                v_wq = 0
                for ic in range(8):
                    v_wq = quant_tile(wgqs[gb], ic * 512, src_rr, ic, go * 512, mi,
                                      w_extra=([("tP", wgq_free[gb])] if ic == 0 else ()))
                for t8 in range(8):
                    tg = H * 8 + t8
                    b = t8 % 4
                    last = None
                    for ic in range(8):
                        w = []
                        if ic == 0:
                            w = [("vV", max(v_wq, pdbank_vV[b])),
                                 ("sS", max(pdbank_sS[b], xq_done[tg]))]
                        last = P.add("tensor", lambda tt, ic=ic, b=b, tg=tg, gb=gb: tt.matmul(
                            pd[b][:, 0:512],
                            xq_fm[:, ic * TOWN + tg * 128: ic * TOWN + tg * 128 + 128],
                            wgqs[gb][:, ic * 512: ic * 512 + 512],
                            start=(ic == 0), stop=(ic == 7), skip_group_check=True),
                            wait=w, inc=("tP" if ic == 7 else None))
                    if gu == 0:
                        sl = P.add("scalar", lambda s, b=b, t8=t8, go=go, tg=tg: s.activation(
                            hbr[:, t8, go * 512:go * 512 + 512],
                            pd[b][:], AF.Silu, scale=sgc[:, tg:tg + 1]),
                            wait=[("tP", last)], inc="sS")
                        pdbank_sS[b], pdbank_vV[b] = sl, 0
                        htile_vV[t8] = 0
                    else:
                        wv = [("tP", last)]
                        if go == 0 and t8 == 0:
                            wv.append(("sS", P.val("sS")))
                        v1 = P.add("vector", lambda v, b=b, t8=t8, go=go: v.tensor_tensor(
                            hbr[:, t8, go * 512:go * 512 + 512],
                            hbr[:, t8, go * 512:go * 512 + 512],
                            pd[b][:], ALU.mult), wait=wv, inc="vV")
                        pdbank_vV[b], pdbank_sS[b] = v1, 0
                        htile_vV[t8] = v1
                wgq_free[gb] = P.val("tP")
        for t8 in range(8):
            tg = H * 8 + t8
            k = t8 % 2
            vop(lambda v, t8=t8: v.tensor_reduce(
                small[:, 58:59], hbr[:, t8, :], axis=AX.X, op=ALU.max,
                apply_absolute_value=True),
                wait=[("vV", htile_vV[t8])])
            vop(lambda v: v.tensor_scalar(
                small[:, 59:60], small[:, 58:59], 1e-5, None, ALU.max))
            vop(lambda v: v.reciprocal(small[:, 60:61], small[:, 59:60]))
            vop(lambda v: v.tensor_scalar(
                small[:, 61:62], small[:, 60:61], 127.0, None, ALU.mult))
            vop(lambda v, tg=tg: v.tensor_tensor(
                small[:, 62:63], small[:, 59:60], rsx[:, tg:tg + 1], ALU.mult))
            vop(lambda v: v.tensor_tensor(
                small[:, 62:63], small[:, 62:63], wscl[:, 6:7], ALU.mult))
            for icg in range(8):
                b = 4 + (icg % 2)
                vop(lambda v, t8=t8, icg=icg: v.tensor_scalar(
                    wqi[:], hbr[:, t8, icg * 512:(icg + 1) * 512], small[:, 61:62],
                    None, ALU.mult))
                v_hq = vop(lambda v: v.tensor_copy(xqb[:, 0:512], wqi[:]))
                last = None
                for k4 in range(4):
                    w = [("vV", v_hq), ("sS", pdbank_sS[b])] if k4 == 0 else []
                    last = P.add("tensor", lambda tt, k4=k4, b=b: tt.matmul(
                        pd[b][:, k4 * 128:(k4 + 1) * 128],
                        xqb[:, k4 * 128:(k4 + 1) * 128], ident[:],
                        start=(k4 == 0), stop=(k4 == 3), skip_group_check=True),
                        wait=w, inc=("tP" if k4 == 3 else None))
                pdbank_sS[b] = P.add("scalar", lambda s, icg=icg, b=b, k=k: s.copy(
                    hq_fm[k][:, icg * 512:(icg + 1) * 512], pd[b][:]),
                    wait=[("tP", last), ("tP", hqfm_free[0])], inc="sS")
            evs = pdbank_sS[4 + ((8 - 1) % 2)]
            evs = max(pdbank_sS[4], pdbank_sS[5])
            smx = "dA" if k == 0 else "dB"
            ldx = P.add("sync", lambda s, tg=tg, k=k: s.dma_start(
                out=x2l[k][:], in_=x2d_t[tg]),
                wait=[("vV", x2l_free[k]), ("dS", P.val("dS"))], inc=smx, n=16)
            for ho in range(2):
                b = ho
                last = None
                for ic in range(32):
                    w = []
                    if ic == 0:
                        w = [("sS", max(evs, pdbank_sS[b])),
                             ("vV", max(v_wdq, pdbank_vV[b]))]
                    last = P.add("tensor", lambda tt, ic=ic, b=b, ho=ho, k=k: tt.matmul(
                        pd[b][:, 0:512],
                        hq_fm[k][:, ic * 128:(ic + 1) * 128],
                        wdq[:, ic * 1024 + ho * 512: ic * 1024 + ho * 512 + 512],
                        start=(ic == 0), stop=(ic == 31), skip_group_check=True),
                        wait=w, inc=("tP" if ic == 31 else None))
                wv = [("tP", last), (smx, ldx)]
                ev = P.add("vector", lambda v, b=b, ho=ho, k=k: v.scalar_tensor_tensor(
                    x2l[k][:, ho * 512:(ho + 1) * 512], pd[b][:],
                    small[:, 62:63], x2l[k][:, ho * 512:(ho + 1) * 512],
                    ALU.mult, ALU.add),
                    wait=wv, inc="vV")
                pdbank_vV[b], pdbank_sS[b] = ev, 0
            hqfm_free[0] = P.val("tP")
            x2l_free[k] = P.val("vV")
            P.add("sync", lambda s, tg=tg, k=k: s.dma_start(
                out=yout_t[tg], in_=x2l[k][:]),
                wait=[("vV", P.val("vV"))], inc="dS", n=16)

    P.add("sync", None, wait=[("dS", P.val("dS")), ("dX", P.val("dX"))])
    P.add("gpsimd", None, wait=[("dS", P.val("dS"))])

    with nc.Block() as block:
        P.emit(block, sems)
    psD.close()
    dstk.close()
    top.close()
    return nc


_NC_CACHE = None


def kernel(L0_attn_hidden_states, L0_attn_Wq, L0_attn_Wk, L0_attn_Wv, L0_attn_Wo,
           L0_ffn_w_gate, L0_ffn_w_up, L0_ffn_w_down):
    global _NC_CACHE
    x = np.asarray(L0_attn_hidden_states, dtype=np.float32)
    wq = np.ascontiguousarray(np.asarray(L0_attn_Wq, np.float32).T)
    wk = np.ascontiguousarray(np.asarray(L0_attn_Wk, np.float32).T)
    wv = np.ascontiguousarray(np.asarray(L0_attn_Wv, np.float32).T)
    wo = np.ascontiguousarray(np.asarray(L0_attn_Wo, np.float32).T)
    wg = np.ascontiguousarray(np.asarray(L0_ffn_w_gate, np.float32).T)
    wu = np.ascontiguousarray(np.asarray(L0_ffn_w_up, np.float32).T)
    wd = np.ascontiguousarray(np.asarray(L0_ffn_w_down, np.float32).T)

    if _NC_CACHE is None:
        _NC_CACHE = build_nc()
    nc = _NC_CACHE

    in_maps = []
    perms = []
    for c in range(8):
        b, p = c // 2, c % 2
        tiles = list(range(p, 32, 2)) + list(range(1 - p, 32, 2))
        rows = np.concatenate([np.arange(t * 128, (t + 1) * 128) for t in tiles])
        perms.append(rows)
        xpm = np.ascontiguousarray(x[b][rows])
        mp = np.full((128, 128), float(p), np.float32)
        in_maps.append({"xp": xpm, "wqT": wq, "wkT": wk, "wvT": wv, "woT": wo,
                        "wgT": wg, "wuT": wu, "wdT": wd, "mpart": mp})

    res = run_bass_kernel_spmd(nc, in_maps, core_ids=list(range(8)))
    out = np.empty_like(x)
    for c in range(8):
        b = c // 2
        own = perms[c][:TOWN]
        out[b][own] = res.results[c]["yout"]
    return out


# version 15 (final)
# speedup vs baseline: 1.0665x; 1.0049x over previous
import numpy as np
import concourse.bass as bass
import concourse.mybir as mybir
from concourse.bass_utils import run_bass_kernel_spmd
from contextlib import ExitStack

F32 = mybir.dt.float32
BF16 = mybir.dt.bfloat16
I8 = mybir.dt.int8
AF = mybir.ActivationFunctionType
ALU = mybir.AluOpType
AX = mybir.AxisListType

D = 1024
T = 4096
TOWN = 2048
HD = 64
INNER = 4096
EPS_LN = 1e-5
EPS_AT = 1e-6
QSCALE = HD ** -0.5
CLIPW = 1.4999999

ENGINES = ("sync", "tensor", "vector", "scalar", "gpsimd")


class Prog:
    def __init__(self):
        self.ops = {e: [] for e in ENGINES}
        self.cnt = {}

    def newsem(self, name):
        self.cnt[name] = 0

    def add(self, eng, fn=None, wait=(), inc=None, n=1):
        out = None
        incs = ()
        if inc is not None:
            self.cnt[inc] += n
            out = self.cnt[inc]
            incs = ((inc, n),)
        wait = tuple((s, v) for (s, v) in wait if v and v > 0)
        self.ops[eng].append((fn, wait, incs))
        return out

    def val(self, sem):
        return self.cnt[sem]

    def emit(self, block, sems):
        def mk(eng_name):
            ops = self.ops[eng_name]

            def body(eng):
                for fn, waits, incs in ops:
                    for s, v in waits:
                        eng.wait_ge(sems[s], v)
                    instr = fn(eng) if fn is not None else None
                    for s, n in incs:
                        instr.then_inc(sems[s], n)
            return body
        block.sync(mk("sync"))
        block.tensor(mk("tensor"))
        block.vector(mk("vector"))
        block.scalar(mk("scalar"))
        block.gpsimd(mk("gpsimd"))
        self.ops = {e: [] for e in ENGINES}


def build_nc():
    nc = bass.Bass("TRN2", target_bir_lowering=False, debug=False)

    xp = nc.dram_tensor("xp", [T, D], F32, kind="ExternalInput").ap()
    wqT = nc.dram_tensor("wqT", [D, D], F32, kind="ExternalInput").ap()
    wkT = nc.dram_tensor("wkT", [D, D], F32, kind="ExternalInput").ap()
    wvT = nc.dram_tensor("wvT", [D, D], F32, kind="ExternalInput").ap()
    woT = nc.dram_tensor("woT", [D, D], F32, kind="ExternalInput").ap()
    wgTd = nc.dram_tensor("wgT", [D, INNER], F32, kind="ExternalInput").ap()
    wuTd = nc.dram_tensor("wuT", [D, INNER], F32, kind="ExternalInput").ap()
    wdTd = nc.dram_tensor("wdT", [INNER, D], F32, kind="ExternalInput").ap()
    mpin = nc.dram_tensor("mpart", [128, 128], F32, kind="ExternalInput").ap()
    yout = nc.dram_tensor("yout", [TOWN, D], F32, kind="ExternalOutput").ap()
    x2d = nc.dram_tensor("x2d", [TOWN, D], F32, kind="Internal")

    xp_t = xp.rearrange("(t p) d -> t p d", p=128)
    yout_t = yout.rearrange("(t p) d -> t p d", p=128)
    x2d_t = x2d.ap().rearrange("(t p) d -> t p d", p=128)
    wq_r = wqT.rearrange("(c p) f -> p c f", p=128)
    wk_r = wkT.rearrange("(c p) f -> p c f", p=128)
    wv_r = wvT.rearrange("(c p) f -> p c f", p=128)
    wo_r = woT.rearrange("(c p) f -> p c f", p=128)
    wg_r = wgTd.rearrange("(c p) f -> c p f", p=128)
    wu_r = wuTd.rearrange("(c p) f -> c p f", p=128)
    wd_r = wdTd.rearrange("(c p) f -> c p f", p=128)

    P = Prog()
    SEMS = ("dS", "dG", "tP", "vV", "sS", "dA", "dB", "dW0", "dW1",
            "dR0", "dR1", "dX")
    for s in SEMS:
        P.newsem(s)

    top = ExitStack()
    E = top.enter_context
    sems = {s: E(nc.semaphore(s)) for s in SEMS}

    ident = E(nc.sbuf_tensor("ident", [128, 128], BF16))
    mtri = E(nc.sbuf_tensor("mtri", [128, 128], BF16))
    mpart = E(nc.sbuf_tensor("mpart_sb", [128, 128], BF16))
    ones_r = E(nc.sbuf_tensor("ones_r", [1, 128], BF16))
    ones_c = E(nc.sbuf_tensor("ones_c", [128, 1], BF16))
    ones_cf = E(nc.sbuf_tensor("ones_cf", [128, 1], F32))
    ones_rf = E(nc.sbuf_tensor("ones_rf", [1, 128], F32))
    wstat = E(nc.sbuf_tensor("wstat", [128, 96], F32))
    wscl = E(nc.sbuf_tensor("wscl", [128, 8], F32))
    small = E(nc.sbuf_tensor("small", [128, 64], F32))
    rsx = E(nc.sbuf_tensor("rsx", [128, 16], F32))
    sgc = E(nc.sbuf_tensor("sgc", [128, 16], F32))
    cst = E(nc.sbuf_tensor("cst", [128, 2], F32))
    ones_sq = E(nc.sbuf_tensor("ones_sq", [128, 128], F32))

    abc = ExitStack()
    EB = abc.enter_context
    iof = EB(nc.sbuf_tensor("iof", [128, 128], BF16))
    iop = EB(nc.sbuf_tensor("iop", [128, 128], BF16))
    xn_fm = EB(nc.sbuf_tensor("xn_fm", [128, 8 * T], BF16))
    OF = EB(nc.sbuf_tensor("OF", [128, 8 * TOWN], BF16))
    K_sb = EB(nc.sbuf_tensor("K_sb", [128, 2 * T], BF16))
    Q_sb = EB(nc.sbuf_tensor("Q_sb", [128, 2 * TOWN], BF16))
    V1 = EB(nc.sbuf_tensor("V1", [128, 32 * 260], BF16))
    Pbuf = EB(nc.sbuf_tensor("Pbuf", [128, 6 * 512], BF16))
    wq_sl = EB(nc.sbuf_tensor("wq_sl", [128, 8 * 256], BF16))
    wk_sl = EB(nc.sbuf_tensor("wk_sl", [128, 8 * 256], BF16))
    wv_sl = EB(nc.sbuf_tensor("wv_sl", [128, 8 * 256], BF16))
    zrow = EB(nc.sbuf_tensor("zrow", [1, 2 * 512], BF16))
    vcs = EB(nc.sbuf_tensor("vcs", [1, 16 * 260], BF16))
    rz_sb = EB(nc.sbuf_tensor("rz_sb", [128, 512], F32))
    xld = [EB(nc.sbuf_tensor(f"xld{k}", [128, D], F32)) for k in range(2)]
    xnt = [EB(nc.sbuf_tensor(f"xnt{k}", [128, D], BF16)) for k in range(2)]
    st_s = EB(nc.sbuf_tensor("st_s", [128, 32], F32))
    st_m = EB(nc.sbuf_tensor("st_m", [128, 32], F32))
    st_q = EB(nc.sbuf_tensor("st_q", [128, 32], F32))
    st_r = EB(nc.sbuf_tensor("st_r", [128, 32], F32))
    wmb = [EB(nc.sbuf_tensor(f"wmb{k}", [128, 1024], F32)) for k in range(2)]
    xres = [EB(nc.sbuf_tensor(f"xres{k}", [128, D], F32)) for k in range(2)]
    x2t = [EB(nc.sbuf_tensor(f"x2t{k}", [128, D], F32)) for k in range(2)]

    psABC = ExitStack()
    EP = psABC.enter_context
    ps_s = [EP(nc.psum_tensor(f"ps_s{i}", [128, 512], F32)) for i in range(4)]
    ps_o = [EP(nc.psum_tensor(f"ps_o{i}", [65, 512], F32)) for i in range(2)]
    ps_v = EP(nc.psum_tensor("ps_v", [1, 260], F32))
    ps_y = EP(nc.psum_tensor("ps_y", [128, 512], F32))

    sbank_sS = [0, 0, 0, 0]
    sbank_vV = [0, 0, 0, 0]
    obank_free = [0, 0]

    P.add("gpsimd", lambda g: g.iota(iof[:], [[1, 128]], channel_multiplier=0,
                                     allow_small_or_imprecise_dtypes=True))
    P.add("gpsimd", lambda g: g.iota(iop[:], [[0, 128]], channel_multiplier=1,
                                     allow_small_or_imprecise_dtypes=True),
          inc="dG", n=16)
    v_io = P.val("dG")
    v_mp = P.add("gpsimd", lambda g: g.dma_start(out=mpart[:], in_=mpin), inc="dG", n=16)
    v_cst = P.add("vector", lambda v: v.memset(cst[:, 0:1], EPS_LN), inc="vV")
    P.add("vector", lambda v: v.memset(ones_sq[:], 1.0), inc="vV")
    P.add("vector", lambda v: v.memset(ones_r[:], 1.0), inc="vV")
    P.add("vector", lambda v: v.memset(ones_c[:], 1.0), inc="vV")
    P.add("vector", lambda v: v.memset(ones_cf[:], 1.0), inc="vV")
    P.add("vector", lambda v: v.memset(ones_rf[:], 1.0), inc="vV")
    P.add("vector", lambda v: v.tensor_tensor(ident[:], iof[:], iop[:], ALU.is_equal),
          wait=[("dG", v_io)], inc="vV")
    v_const_all = P.add("vector", lambda v: v.tensor_tensor(mtri[:], iop[:], iof[:], ALU.is_le),
                        inc="vV")

    xn_ready = [0] * 2
    xn_freed = 0
    xld_freed = [0, 0]
    stats_state = {}
    vchain = [0]

    def vop(fn, wait=()):
        w = list(wait) + [("vV", vchain[0])]
        vchain[0] = P.add("vector", fn, wait=w, inc="vV")
        return vchain[0]


    def emit_stats(t):
        kx = t % 2
        k4 = t % 2
        sm = "dA" if t % 2 == 0 else "dB"
        ld = P.add("sync", lambda s, kx=kx, t=t: s.dma_start(
            out=xld[kx][:], in_=xp_t[t]),
            wait=[("vV", xld_freed[kx])], inc=sm, n=16)
        v_red = vop(lambda v, kx=kx, t=t: v.tensor_reduce(
            st_s[:, t:t + 1], xld[kx][:], axis=AX.X, op=ALU.add),
            wait=[(sm, ld)])
        vop(lambda v, t=t: v.tensor_scalar(
            st_m[:, t:t + 1], st_s[:, t:t + 1], -1.0 / D, None, ALU.mult))
        vop(lambda v, kx=kx, k4=k4, t=t: v.tensor_scalar(
            xnt[k4][:], xld[kx][:], st_m[:, t:t + 1], None, ALU.add),
            wait=[("tP", xn_freed)])
        xld_freed[kx] = vop(lambda v, k4=k4: v.tensor_tensor(
            wmb[0][:], xnt[k4][:], xnt[k4][:], ALU.mult))
        vop(lambda v, t=t: v.tensor_reduce(
            st_q[:, t:t + 1], wmb[0][:], axis=AX.X, op=ALU.add))
        v_var = vop(lambda v, t=t: v.tensor_scalar(
            st_q[:, t:t + 1], st_q[:, t:t + 1], 1.0 / D, None, ALU.mult))
        stats_state[t] = (kx, v_var, 0)

    def emit_finish(t):
        kx, v_var, _ = stats_state.pop(t)
        k4 = t % 2
        s_std = P.add("scalar", lambda s, t=t: s.activation(
            st_s[:, t:t + 1], st_q[:, t:t + 1], AF.Sqrt, bias=cst[:, 0:1]),
            wait=[("vV", max(v_var, P.val("vV"))), ("vV", v_cst)], inc="sS")
        vop(lambda v, t=t: v.reciprocal(
            st_r[:, t:t + 1], st_s[:, t:t + 1]),
            wait=[("sS", s_std)])
        xn_ready[k4] = vop(lambda v, k4=k4, t=t: v.tensor_scalar(
            xnt[k4][:], xnt[k4][:], st_r[:, t:t + 1], None, ALU.mult))

    def emit_transposes(tg):
        for f in range(8):
            b = f % 2
            last = None
            for k4 in range(2):
                w = [("vV", max(xn_ready + [v_const_all])), ("sS", sbank_sS[b])] if k4 == 0 else []
                last = P.add("tensor", lambda tt, f=f, k4=k4, b=b: tt.matmul(
                    ps_s[b][:, k4 * 128:(k4 + 1) * 128],
                    xnt[k4][:, f * 128:(f + 1) * 128], ident[:],
                    start=(k4 == 0), stop=(k4 == 1), skip_group_check=True),
                    wait=w, inc=("tP" if k4 == 1 else None))
            sbank_sS[b] = P.add("scalar", lambda s, f=f, tg=tg, b=b: s.copy(
                xn_fm[:, f * T + tg * 256: f * T + tg * 256 + 256],
                ps_s[b][:, 0:256]),
                wait=[("tP", last)], inc="sS")
        return P.val("tP")

    for idx in range(33):
        if idx >= 1:
            t = idx - 1
            emit_finish(t)
            if t % 2 == 1:
                xn_freed = emit_transposes(t // 2)
        if idx < 32:
            emit_stats(idx)
    xnfm_all = P.val("sS")

    V1r = V1[:].rearrange("p (s c) -> p s c", c=260)
    v_v1ones = None
    for h in range(4):
        v_v1ones = P.add("vector", lambda v, h=h: v.memset(
            V1r[:, :, 65 * h + 64:65 * h + 65], 1.0), inc="vV")

    wmean_tiles = []
    for (rr, nch) in ((wg_r, 8), (wu_r, 8), (wd_r, 32)):
        for ci in range(nch):
            nb = 4 if rr is not wd_r else 1
            for fb in range(nb):
                wmean_tiles.append((rr, ci, fb * 1024))
    assert len(wmean_tiles) == 96
    wm_abs = {}

    def emit_wmean():
        idx = len(wm_abs)
        if idx >= 96:
            return
        rr, ci, f0 = wmean_tiles[idx]
        buf = wmb[idx % 2]
        sm = "dW0" if idx % 2 == 0 else "dW1"
        w = [("vV", wm_abs[idx - 2])] if idx >= 2 else [("sS", xnfm_all)]
        ld = P.add("sync", lambda s, rr=rr, ci=ci, f0=f0, buf=buf: s.dma_start(
            out=buf[:], in_=rr[ci][:, f0:f0 + 1024]), wait=w, inc=sm, n=16)
        wm_abs[idx] = P.add("vector", lambda v, buf=buf, idx=idx: v.tensor_reduce(
            wstat[:, idx:idx + 1], buf[:], axis=AX.X, op=ALU.add,
            apply_absolute_value=True),
            wait=[(sm, ld)], inc="vV")

    bank_rr = [0]
    prev_av_last = [0]
    pending_norm = []
    norm_done = {}
    zslot_free = [0]
    zb_free = [0]

    def emit_norm():
        if not pending_norm:
            return
        cp, J, i2, e4 = pending_norm.pop(0)
        P.add("tensor", lambda tt, i2=i2: tt.matmul(
            ps_y[0:64, :], ones_r[0:1, 0:64],
            zrow[0:1, (2 * i2) * 512:(2 * i2) * 512 + 512],
            start=True, stop=False, skip_group_check=True),
            wait=[("sS", e4), ("vV", zb_free[0])])
        mmz = P.add("tensor", lambda tt, i2=i2: tt.matmul(
            ps_y[64:128, :], ones_r[0:1, 0:64],
            zrow[0:1, (2 * i2 + 1) * 512:(2 * i2 + 1) * 512 + 512],
            start=True, stop=True, skip_group_check=True), inc="tP")
        zslot_free[i2] = mmz
        zb_free[0] = P.add("vector", lambda v: v.tensor_scalar(
            rz_sb[:], ps_y[:], 2 * EPS_AT, None, ALU.add),
            wait=[("tP", mmz)], inc="vV")
        P.add("vector", lambda v: v.reciprocal(rz_sb[:], rz_sb[:]))
        norm_done[(cp, J)] = P.add("vector", lambda v, cp=cp, J=J: v.tensor_tensor(
            OF[:, cp * TOWN + J * 512: cp * TOWN + J * 512 + 512],
            OF[:, cp * TOWN + J * 512: cp * TOWN + J * 512 + 512],
            rz_sb[:], ALU.mult), inc="vV")

    v_mp_safe = None
    for g in range(4):
        wsl_ld = None
        w0 = [("tP", P.val("tP"))] if g > 0 else []
        for (dst, src) in ((wq_sl, wq_r), (wk_sl, wk_r), (wv_sl, wv_r)):
            wsl_ld = P.add("gpsimd", lambda gg, dst=dst, src=src, g=g: gg.dma_start(
                out=dst[:].rearrange("p (c f) -> p c f", f=256),
                in_=src[:, :, g * 256:(g + 1) * 256]),
                wait=w0, inc="dG", n=16)
            w0 = []
        if g == 0:
            v_mp_safe = wsl_ld

        def proj_tile(lhs_sl, wcol, rhs_off, width, first_w, ev_fn, ev_eng):
            b = bank_rr[0] % 4
            bank_rr[0] += 1
            last = None
            for ic in range(8):
                w = []
                if ic == 0:
                    w = [("sS", sbank_sS[b]), ("vV", sbank_vV[b])] + first_w
                last = P.add("tensor", lambda tt, ic=ic, b=b, lhs_sl=lhs_sl, wcol=wcol,
                             rhs_off=rhs_off, width=width: tt.matmul(
                    ps_s[b][:, 0:width],
                    lhs_sl[:, ic * 256 + wcol: ic * 256 + wcol + 128],
                    xn_fm[:, ic * T + rhs_off: ic * T + rhs_off + width],
                    start=(ic == 0), stop=(ic == 7), skip_group_check=True),
                    wait=w, inc=("tP" if ic == 7 else None))
            ev = P.add(ev_eng, lambda e, b=b: ev_fn(e, ps_s[b]),
                       wait=[("tP", last)],
                       inc=("sS" if ev_eng == "scalar" else "vV"))
            if ev_eng == "scalar":
                sbank_sS[b], sbank_vV[b] = ev, 0
            else:
                sbank_sS[b], sbank_vV[b] = 0, ev

        ei = 0
        for kc in range(2):
            for t8 in range(8):
                fw = [("dG", wsl_ld), ("sS", xnfm_all)] if ei == 0 else []
                dsto = kc * T + t8 * 512
                if ei % 2 == 0:
                    proj_tile(wk_sl, kc * 128, t8 * 512, 512, fw,
                              lambda e, psb, dsto=dsto: e.copy(
                                  K_sb[:, dsto:dsto + 512], psb[:, 0:512]), "scalar")
                else:
                    proj_tile(wk_sl, kc * 128, t8 * 512, 512, fw,
                              lambda e, psb, dsto=dsto: e.tensor_copy(
                                  K_sb[:, dsto:dsto + 512], psb[:, 0:512]), "vector")
                ei += 1
        for kc in range(2):
            for t4 in range(4):
                dsto = kc * TOWN + t4 * 512
                proj_tile(wq_sl, kc * 128, t4 * 512, 512, [],
                          lambda e, psb, dsto=dsto: e.tensor_scalar(
                              Q_sb[:, dsto:dsto + 512], psb[:, 0:512],
                              QSCALE, None, ALU.mult), "vector")
                ei += 1
        for t1 in range(32):
            b = bank_rr[0] % 4
            bank_rr[0] += 1
            last = None
            for ic in range(8):
                w = [("sS", sbank_sS[b]), ("vV", sbank_vV[b])] if ic == 0 else []
                last = P.add("tensor", lambda tt, ic=ic, b=b, t1=t1: tt.matmul(
                    ps_s[b][:, 0:256],
                    xn_fm[:, ic * T + t1 * 128: ic * T + t1 * 128 + 128],
                    wv_sl[:, ic * 256: ic * 256 + 256],
                    start=(ic == 0), stop=(ic == 7), skip_group_check=True),
                    wait=w, inc=("tP" if ic == 7 else None))
            evv = None
            for h in range(4):
                w = [("tP", last)] if h == 0 else []
                if g == 0 and t1 == 0 and h == 0:
                    w.append(("vV", v_v1ones))
                evv = P.add("vector", lambda v, b=b, t1=t1, h=h: v.tensor_copy(
                    V1[:, t1 * 260 + 65 * h: t1 * 260 + 65 * h + 64],
                    ps_s[b][:, h * 64:(h + 1) * 64]),
                    wait=w, inc="vV")
            sbank_sS[b], sbank_vV[b] = 0, evv
            if t1 % 2 == 1:
                emit_wmean()
        v_proj_sS = P.val("sS")
        v_proj_vV = P.val("vV")

        last_copy = 0
        for t in range(1, 16):
            w = ([("vV", v_proj_vV), ("sS", v_proj_sS)] if t == 1
                 else [("sS", last_copy)])
            P.add("tensor", lambda tt, t=t: tt.matmul(
                ps_v[:], ones_c[:], V1[:, (t - 1) * 260:(t - 1) * 260 + 260],
                start=(t == 1), stop=False, skip_group_check=True), wait=w)
            mm2 = P.add("tensor", lambda tt, t=t: tt.matmul(
                ps_v[:], ones_c[:], V1[:, (16 + t - 1) * 260:(16 + t - 1) * 260 + 260],
                start=False, stop=(t == 15), skip_group_check=True), inc="tP")
            last_copy = P.add("scalar", lambda s, t=t: s.copy(
                vcs[0:1, t * 260:(t + 1) * 260], ps_v[:]),
                wait=[("tP", mm2)], inc="sS")
        v_snap = last_copy

        for lp in range(2):
            cp = 2 * g + lp
            for J in range(4):
                emit_wmean()
                slots = list(range(4 * J + 4)) + [16 + r for r in range(4 * J + 4)]
                ns = len(slots)
                pA, pB, sttA, sttB, avB, smm = {}, {}, {}, {}, {}, {}

                def c0_of(sl):
                    return 128 * max(0, (sl % 16) - 4 * J)

                def emit_av(si, lp=lp, J=J, slots=slots, sttA=sttA, sttB=sttB,
                            pA=pA, pB=pB, avB=avB):
                    sl = slots[si]
                    c0 = 128 * max(0, (sl % 16) - 4 * J)
                    bb = si % 3
                    boundary = (sl % 16) >= 4 * J
                    wA = [("vV", sttB[si])] if boundary else [("sS", pA[si])]
                    if si == 0:
                        wA += [("sS", obank_free[0]), ("sS", obank_free[1])]
                    P.add("tensor", lambda tt, sl=sl, c0=c0, bb=bb, si=si, lp=lp: tt.matmul(
                        ps_o[0][:, c0:512],
                        V1[:, sl * 260 + 65 * (2 * lp): sl * 260 + 65 * (2 * lp) + 65],
                        Pbuf[:, (2 * bb) * 512 + c0:(2 * bb) * 512 + 512],
                        start=(si == 0), stop=False, skip_group_check=True),
                        wait=wA)
                    wB = [("vV", sttB[si])] if boundary else [("sS", pB[si])]
                    avB[si] = P.add("tensor", lambda tt, sl=sl, c0=c0, bb=bb, si=si, lp=lp: tt.matmul(
                        ps_o[1][:, c0:512],
                        V1[:, sl * 260 + 65 * (2 * lp + 1): sl * 260 + 65 * (2 * lp + 1) + 65],
                        Pbuf[:, (2 * bb + 1) * 512 + c0:(2 * bb + 1) * 512 + 512],
                        start=(si == 0), stop=False, skip_group_check=True),
                        wait=wB, inc="tP")

                for si, sl in enumerate(slots):
                    c0 = c0_of(sl)
                    bb = si % 3
                    bA, bB = 2 * (si % 2), 2 * (si % 2) + 1
                    boundary = (sl % 16) >= 4 * J
                    if si < 2:
                        w = [("sS", max(sbank_sS[bA], sbank_sS[bB], v_snap)),
                             ("vV", max(sbank_vV[bA], sbank_vV[bB]))]
                        sbank_sS[bA] = sbank_sS[bB] = 0
                        sbank_vV[bA] = sbank_vV[bB] = 0
                    else:
                        w = [("sS", pB[si - 2])]
                    kcol = lp * T + sl * 128
                    qcol = lp * TOWN + J * 512 + c0
                    P.add("tensor", lambda tt, bA=bA, c0=c0, kcol=kcol, qcol=qcol: tt.matmul(
                        ps_s[bA][:, c0:512], K_sb[0:64, kcol:kcol + 128],
                        Q_sb[0:64, qcol:qcol + 512 - c0],
                        start=True, stop=True, tile_position=(0, 0),
                        skip_group_check=True), wait=w)
                    smm[si] = P.add("tensor", lambda tt, bB=bB, c0=c0, kcol=kcol, qcol=qcol: tt.matmul(
                        ps_s[bB][:, c0:512], K_sb[64:128, kcol:kcol + 128],
                        Q_sb[64:128, qcol:qcol + 512 - c0],
                        start=True, stop=True, tile_position=(64, 0),
                        skip_group_check=True), inc="tP")
                    if si == 2:
                        emit_norm()
                    if si >= 1:
                        emit_av(si - 1)
                    pw = [("tP", smm[si])]
                    if si >= 3:
                        pw.append(("tP", avB[si - 3]))
                    elif prev_av_last[0]:
                        pw.append(("tP", prev_av_last[0]))
                    pA[si] = P.add("scalar", lambda s, bb=bb, bA=bA, c0=c0: s.activation(
                        Pbuf[:, (2 * bb) * 512 + c0:(2 * bb) * 512 + 512],
                        ps_s[bA][:, c0:512], AF.Square, bias=1.0),
                        wait=pw, inc="sS")
                    pB[si] = P.add("scalar", lambda s, bb=bb, bB=bB, c0=c0: s.activation(
                        Pbuf[:, (2 * bb + 1) * 512 + c0:(2 * bb + 1) * 512 + 512],
                        ps_s[bB][:, c0:512], AF.Square, bias=1.0),
                        inc="sS")
                    if boundary:
                        mk_ = mtri if sl < 16 else mpart
                        mw = [("sS", pA[si])]
                        if sl >= 16:
                            mw.append(("dG", v_mp_safe))
                        sttA[si] = P.add("vector", lambda v, bb=bb, c0=c0, mk_=mk_: v.scalar_tensor_tensor(
                            Pbuf[:, (2 * bb) * 512 + c0:(2 * bb) * 512 + c0 + 128],
                            Pbuf[:, (2 * bb) * 512 + c0:(2 * bb) * 512 + c0 + 128],
                            1.0, mk_[:], ALU.add, ALU.mult), wait=mw, inc="vV")
                        sttB[si] = P.add("vector", lambda v, bb=bb, c0=c0, mk_=mk_: v.scalar_tensor_tensor(
                            Pbuf[:, (2 * bb + 1) * 512 + c0:(2 * bb + 1) * 512 + c0 + 128],
                            Pbuf[:, (2 * bb + 1) * 512 + c0:(2 * bb + 1) * 512 + c0 + 128],
                            1.0, mk_[:], ALU.add, ALU.mult),
                            wait=[("sS", pB[si])], inc="vV")
                emit_av(ns - 1)
                sbank_sS[0] = sbank_sS[1] = pB[ns - 2]
                sbank_sS[2] = sbank_sS[3] = pB[ns - 1]
                prev_av_last[0] = avB[ns - 1]
                lastk = avB[ns - 1]
                for i in range(4):
                    t = 4 * J + i
                    if t == 0:
                        continue
                    for oi, hl in ((0, 2 * lp), (1, 2 * lp + 1)):
                        stop = (i == 3) and (oi == 1)
                        lastk = P.add("tensor", lambda tt, t=t, oi=oi, hl=hl, i=i, stop=stop: tt.matmul(
                            ps_o[oi][:, i * 128:(i + 1) * 128],
                            vcs[0:1, t * 260 + 65 * hl: t * 260 + 65 * hl + 65],
                            ones_r[:, 0:128],
                            start=False, stop=stop, skip_group_check=True),
                            inc=("tP" if stop else None))
                i2 = 0
                P.add("scalar", lambda s, cp=cp, J=J: s.copy(
                    OF[0:64, cp * TOWN + J * 512: cp * TOWN + J * 512 + 512],
                    ps_o[0][0:64, :]), wait=[("tP", lastk)], inc="sS")
                P.add("scalar", lambda s, cp=cp, J=J: s.copy(
                    OF[64:128, cp * TOWN + J * 512: cp * TOWN + J * 512 + 512],
                    ps_o[1][0:64, :]), inc="sS")
                e3 = P.add("scalar", lambda s, i2=i2: s.copy(
                    zrow[0:1, (2 * i2) * 512:(2 * i2) * 512 + 512],
                    ps_o[0][64:65, :]),
                    wait=[("tP", zslot_free[i2])], inc="sS")
                e4 = P.add("scalar", lambda s, i2=i2: s.copy(
                    zrow[0:1, (2 * i2 + 1) * 512:(2 * i2 + 1) * 512 + 512],
                    ps_o[1][64:65, :]), inc="sS")
                obank_free[0], obank_free[1] = e3, e4
                pending_norm.append((cp, J, i2, e4))
    while pending_norm:
        emit_norm()
    while len(wm_abs) < 96:
        emit_wmean()
    v_wmean_all = wm_abs[95]

    v_wo = P.add("gpsimd", lambda gg: gg.dma_start(
        out=K_sb[:].rearrange("p (c f) -> p c f", f=1024), in_=wo_r[:, :, :]),
        wait=[("tP", P.val("tP"))], inc="dG", n=16)

    for mi, (a, bcnt) in enumerate(((0, 32), (32, 32), (64, 32))):
        vop(lambda v, mi=mi, a=a, bcnt=bcnt: v.tensor_reduce(
            small[:, 32 + mi:33 + mi], wstat[:, a:a + bcnt], axis=AX.X, op=ALU.add),
            wait=[("vV", v_wmean_all)])
    vop(lambda v: v.tensor_copy(small[:, 36:39], small[:, 32:35]))
    for _ in range(4):
        vop(lambda v: v.tensor_copy(small[:, 0:1], small[:, 36:37]))
    v_cast = vop(lambda v: v.tensor_copy(small[:, 0:1], small[:, 36:37]))
    mmw = P.add("tensor", lambda tt: tt.matmul(
        ps_y[:, 0:3], ones_sq[:], small[:, 36:39],
        start=True, stop=True, skip_group_check=True),
        wait=[("vV", v_cast)], inc="tP")
    nels = [float(D * INNER), float(D * INNER), float(INNER * D)]
    for mi in range(3):
        vop(lambda v, mi=mi: v.tensor_scalar(
            wscl[:, 3 + mi:4 + mi], ps_y[:, mi:mi + 1], 1.0 / nels[mi],
            None, ALU.mult),
            wait=([("tP", mmw)] if mi == 0 else []))
        vop(lambda v, mi=mi: v.tensor_scalar(
            wscl[:, 3 + mi:4 + mi], wscl[:, 3 + mi:4 + mi], 1e-5, None, ALU.max))
        vop(lambda v, mi=mi: v.reciprocal(wscl[:, mi:mi + 1], wscl[:, 3 + mi:4 + mi]))
    vop(lambda v: v.tensor_tensor(
        wscl[:, 6:7], wscl[:, 4:5], wscl[:, 5:6], ALU.mult))
    v_wscl = vop(lambda v: v.tensor_scalar(
        wscl[:, 6:7], wscl[:, 6:7], 1.0 / 127.0, None, ALU.mult))

    xres_free = [0, 0]
    x2_store = {}
    for tt16 in range(16):
        k = tt16 % 2
        smR = "dR0" if k == 0 else "dR1"
        ld = P.add("sync", lambda s, tt16=tt16, k=k: s.dma_start(
            out=xres[k][:], in_=xp_t[tt16]),
            wait=[("vV", xres_free[k])], inc=smR, n=16)
        evs = []
        for half in range(2):
            b = bank_rr[0] % 4
            bank_rr[0] += 1
            last = None
            for oc in range(8):
                w = []
                if oc == 0:
                    w = [("sS", sbank_sS[b]), ("vV", sbank_vV[b]),
                         ("vV", max(norm_done[(cc, tt16 // 4)] for cc in range(8)))]
                    if tt16 == 0 and half == 0:
                        w.append(("dG", v_wo))
                last = P.add("tensor", lambda tt, oc=oc, b=b, tt16=tt16, half=half: tt.matmul(
                    ps_s[b][:, 0:512],
                    OF[:, oc * TOWN + tt16 * 128: oc * TOWN + tt16 * 128 + 128],
                    K_sb[:, oc * 1024 + half * 512: oc * 1024 + half * 512 + 512],
                    start=(oc == 0), stop=(oc == 7), skip_group_check=True),
                    wait=w, inc=("tP" if oc == 7 else None))
            wv = [("tP", last), (smR, ld)]
            if tt16 >= 2:
                wv.append(("dX", x2_store[tt16 - 1]))
            ev = P.add("vector", lambda v, b=b, half=half, k=k: v.tensor_tensor(
                x2t[k][:, half * 512:(half + 1) * 512], ps_s[b][:, 0:512],
                xres[k][:, half * 512:(half + 1) * 512], ALU.add),
                wait=wv, inc="vV")
            evs.append(ev)
            sbank_sS[b], sbank_vV[b] = 0, ev
        xres_free[k] = evs[-1]
        x2_store[tt16] = P.add("sync", lambda s, tt16=tt16, k=k: s.dma_start(
            out=x2d_t[tt16], in_=x2t[k][:]),
            wait=[("vV", evs[-1])], inc="dX", n=16)

    with nc.Block() as block:
        P.emit(block, sems)
    psABC.close()
    abc.close()

    dstk = ExitStack()
    ED = dstk.enter_context
    xq_fm = ED(nc.sbuf_tensor("xq_fm", [128, 8 * TOWN], BF16))
    hbuf = ED(nc.sbuf_tensor("hbuf", [128, 8 * INNER], BF16))
    wdq = ED(nc.sbuf_tensor("wdq", [128, 32 * 1024], BF16))
    wgqs = [ED(nc.sbuf_tensor(f"wgq{k}", [128, 8 * 512], BF16)) for k in range(2)]
    wfs = [ED(nc.sbuf_tensor(f"wfs{k}", [128, 512], F32)) for k in range(2)]
    wqa = ED(nc.sbuf_tensor("wqa", [128, 512], F32))
    wqi = ED(nc.sbuf_tensor("wqi", [128, 512], I8))
    x2l = [ED(nc.sbuf_tensor(f"x2l{k}", [128, D], F32)) for k in range(2)]
    sqs = ED(nc.sbuf_tensor("sqs", [128, D], F32))
    xqb = ED(nc.sbuf_tensor("xqb", [128, D], BF16))
    hq_fm = [ED(nc.sbuf_tensor("hqfm0", [128, INNER], BF16))] * 2

    psD = ExitStack()
    EPD = psD.enter_context
    pd = [EPD(nc.psum_tensor(f"pd{i}", [128, 512], F32)) for i in range(6)]
    pd_w = EPD(nc.psum_tensor("pd_w", [1, 4], F32))
    pd_b = EPD(nc.psum_tensor("pd_b", [128, 1], F32))
    dstat = wstat
    pdbank_sS = [0] * 6
    pdbank_vV = [0] * 6

    x2l_free = [0, 0]
    d0_state = {}

    def d0_stats(t):
        k = t % 2
        sm = "dA" if k == 0 else "dB"
        ld = P.add("sync", lambda s, t=t, k=k: s.dma_start(
            out=x2l[k][:], in_=x2d_t[t]),
            wait=[("dX", x2_store[t]), ("vV", x2l_free[k])], inc=sm, n=16)
        vop(lambda v, k=k, t=t: v.tensor_reduce(
            dstat[:, t:t + 1], x2l[k][:], axis=AX.X, op=ALU.add),
            wait=[(sm, ld)])
        vop(lambda v, t=t: v.tensor_scalar(
            dstat[:, 32 + t:33 + t], dstat[:, t:t + 1], -1.0 / D, None, ALU.mult))
        vop(lambda v, k=k, t=t: v.tensor_scalar(
            x2l[k][:], x2l[k][:], dstat[:, 32 + t:33 + t], None, ALU.add))
        vop(lambda v, k=k: v.tensor_tensor(
            sqs[:], x2l[k][:], x2l[k][:], ALU.mult))
        vop(lambda v, t=t: v.tensor_reduce(
            dstat[:, 16 + t:17 + t], sqs[:], axis=AX.X, op=ALU.add))
        v_var = vop(lambda v, t=t: v.tensor_scalar(
            dstat[:, 16 + t:17 + t], dstat[:, 16 + t:17 + t], 1.0 / D, None,
            ALU.mult))
        d0_state[t] = (k, v_var)

    def d0_finish(t):
        k, v_var = d0_state.pop(t)
        s_std = P.add("scalar", lambda s, t=t: s.activation(
            dstat[:, t:t + 1], dstat[:, 16 + t:17 + t], AF.Sqrt, bias=cst[:, 0:1]),
            wait=[("vV", max(v_var, P.val("vV")))], inc="sS")
        vop(lambda v, t=t: v.reciprocal(
            dstat[:, 48 + t:49 + t], dstat[:, t:t + 1]), wait=[("sS", s_std)])
        vop(lambda v, k=k, t=t: v.tensor_scalar(
            x2l[k][:], x2l[k][:], dstat[:, 48 + t:49 + t], None, ALU.mult))
        vop(lambda v, k=k, t=t: v.tensor_reduce(
            dstat[:, 64 + t:65 + t], x2l[k][:], axis=AX.X, op=ALU.max,
            apply_absolute_value=True))
        vop(lambda v, t=t: v.tensor_scalar(
            small[:, 55:56], dstat[:, 64 + t:65 + t], 1e-5, None, ALU.max))
        vop(lambda v: v.reciprocal(small[:, 56:57], small[:, 55:56]))
        vop(lambda v: v.tensor_scalar(
            small[:, 57:58], small[:, 56:57], 127.0, None, ALU.mult))
        vop(lambda v, t=t: v.tensor_scalar(
            rsx[:, t:t + 1], small[:, 55:56], 1.0 / 127.0, None, ALU.mult))
        vop(lambda v, t=t: v.tensor_tensor(
            sgc[:, t:t + 1], rsx[:, t:t + 1], wscl[:, 3:4], ALU.mult),
            wait=[("vV", v_wscl)])
        for fi in range(2):
            b = 4 + fi
            vm = vop(lambda v, k=k, fi=fi: v.tensor_scalar(
                wqi[:], x2l[k][:, fi * 512:(fi + 1) * 512], small[:, 57:58],
                None, ALU.mult))
            if fi == 1:
                x2l_free[k] = vm
            v_xq = vop(lambda v, fi=fi: v.tensor_copy(
                xqb[:, fi * 512:(fi + 1) * 512], wqi[:]))
            last = None
            for k4 in range(4):
                f = fi * 4 + k4
                w = [("vV", v_xq), ("sS", pdbank_sS[b])] if k4 == 0 else []
                last = P.add("tensor", lambda tt, f=f, k4=k4, b=b: tt.matmul(
                    pd[b][:, k4 * 128:(k4 + 1) * 128],
                    xqb[:, f * 128:(f + 1) * 128], ident[:],
                    start=(k4 == 0), stop=(k4 == 3), skip_group_check=True),
                    wait=w, inc=("tP" if k4 == 3 else None))
            pdbank_sS[b] = P.add("scalar", lambda s, fi=fi, t=t, b=b: s.copy(
                xq_fm[:].rearrange("p (c q) -> p c q", q=TOWN)[
                    :, fi * 4:fi * 4 + 4, t * 128:t * 128 + 128],
                pd[b][:].rearrange("p (c q) -> p c q", q=128)),
                wait=[("tP", last)], inc="sS")
        xq_done[t] = pdbank_sS[5]

    xq_done = {}
    for idx in range(9):
        if idx >= 1:
            d0_finish(idx - 1)
        if idx < 8:
            d0_stats(idx)

    quant_rr = [0]
    wfs_free = [0, 0]

    def quant_tile(dst, dst_off, src_rr, ci, f0, mi, w_extra=()):
        k = quant_rr[0] % 2
        quant_rr[0] += 1
        sm = "dW0" if k == 0 else "dW1"
        w = list(w_extra) + [("vV", wfs_free[k])]
        ld = P.add("sync", lambda s, src_rr=src_rr, ci=ci, f0=f0, k=k: s.dma_start(
            out=wfs[k][:], in_=src_rr[ci][:, f0:f0 + 512]),
            wait=w, inc=sm, n=16)
        v1 = P.add("vector", lambda v, k=k, mi=mi: v.tensor_scalar(
            wqa[:], wfs[k][:], wscl[:, mi:mi + 1], CLIPW, ALU.mult, ALU.min),
            wait=[(sm, ld), ("vV", v_wscl)], inc="vV")
        wfs_free[k] = v1
        P.add("vector", lambda v: v.tensor_scalar(
            wqi[:], wqa[:], -CLIPW, None, ALU.max))
        return P.add("vector", lambda v, dst=dst, dst_off=dst_off: v.tensor_copy(
            dst[:, dst_off:dst_off + 512], wqi[:]), inc="vV")

    v_wdq = 0
    for ci in range(32):
        for fh in range(2):
            v_wdq = quant_tile(wdq, ci * 1024 + fh * 512, wd_r, ci, fh * 512, 2)

    hbr = hbuf[:].rearrange("p (t f) -> p t f", f=INNER)
    wgq_free = [0, 0]
    hqfm_free = [0]
    htile_vV = [0] * 8
    d0_nxt = [8]
    go_idx = [0]
    for H in range(2):
        for gu in range(2):
            mi = gu
            src_rr = wg_r if gu == 0 else wu_r
            for go in range(8):
                gb = go_idx[0] % 2
                go_idx[0] += 1
                if H == 0 and gu == 0 and d0_nxt[0] < 16:
                    t = d0_nxt[0]
                    d0_nxt[0] += 1
                    d0_stats(t)
                    d0_finish(t)
                v_wq = 0
                for ic in range(8):
                    v_wq = quant_tile(wgqs[gb], ic * 512, src_rr, ic, go * 512, mi,
                                      w_extra=([("tP", wgq_free[gb])] if ic == 0 else ()))
                for t8 in range(8):
                    tg = H * 8 + t8
                    b = t8 % 4
                    last = None
                    for ic in range(8):
                        w = []
                        if ic == 0:
                            w = [("vV", max(v_wq, pdbank_vV[b])),
                                 ("sS", max(pdbank_sS[b], xq_done[tg]))]
                        last = P.add("tensor", lambda tt, ic=ic, b=b, tg=tg, gb=gb: tt.matmul(
                            pd[b][:, 0:512],
                            xq_fm[:, ic * TOWN + tg * 128: ic * TOWN + tg * 128 + 128],
                            wgqs[gb][:, ic * 512: ic * 512 + 512],
                            start=(ic == 0), stop=(ic == 7), skip_group_check=True),
                            wait=w, inc=("tP" if ic == 7 else None))
                    if gu == 0:
                        sl = P.add("scalar", lambda s, b=b, t8=t8, go=go, tg=tg: s.activation(
                            hbr[:, t8, go * 512:go * 512 + 512],
                            pd[b][:], AF.Silu, scale=sgc[:, tg:tg + 1]),
                            wait=[("tP", last)], inc="sS")
                        pdbank_sS[b], pdbank_vV[b] = sl, 0
                        htile_vV[t8] = 0
                    else:
                        wv = [("tP", last)]
                        if go == 0 and t8 == 0:
                            wv.append(("sS", P.val("sS")))
                        v1 = P.add("vector", lambda v, b=b, t8=t8, go=go: v.tensor_tensor(
                            hbr[:, t8, go * 512:go * 512 + 512],
                            hbr[:, t8, go * 512:go * 512 + 512],
                            pd[b][:], ALU.mult), wait=wv, inc="vV")
                        pdbank_vV[b], pdbank_sS[b] = v1, 0
                        htile_vV[t8] = v1
                wgq_free[gb] = P.val("tP")
        for t8 in range(8):
            tg = H * 8 + t8
            k = t8 % 2
            vop(lambda v, t8=t8: v.tensor_reduce(
                small[:, 58:59], hbr[:, t8, :], axis=AX.X, op=ALU.max,
                apply_absolute_value=True),
                wait=[("vV", htile_vV[t8])])
            vop(lambda v: v.tensor_scalar(
                small[:, 59:60], small[:, 58:59], 1e-5, None, ALU.max))
            vop(lambda v: v.reciprocal(small[:, 60:61], small[:, 59:60]))
            vop(lambda v: v.tensor_scalar(
                small[:, 61:62], small[:, 60:61], 127.0, None, ALU.mult))
            vop(lambda v, tg=tg: v.tensor_tensor(
                small[:, 62:63], small[:, 59:60], rsx[:, tg:tg + 1], ALU.mult))
            vop(lambda v: v.tensor_tensor(
                small[:, 62:63], small[:, 62:63], wscl[:, 6:7], ALU.mult))
            for icg in range(8):
                b = 4 + (icg % 2)
                vop(lambda v, t8=t8, icg=icg: v.tensor_scalar(
                    wqi[:], hbr[:, t8, icg * 512:(icg + 1) * 512], small[:, 61:62],
                    None, ALU.mult))
                v_hq = vop(lambda v: v.tensor_copy(xqb[:, 0:512], wqi[:]))
                last = None
                for k4 in range(4):
                    w = [("vV", v_hq), ("sS", pdbank_sS[b])] if k4 == 0 else []
                    last = P.add("tensor", lambda tt, k4=k4, b=b: tt.matmul(
                        pd[b][:, k4 * 128:(k4 + 1) * 128],
                        xqb[:, k4 * 128:(k4 + 1) * 128], ident[:],
                        start=(k4 == 0), stop=(k4 == 3), skip_group_check=True),
                        wait=w, inc=("tP" if k4 == 3 else None))
                pdbank_sS[b] = P.add("scalar", lambda s, icg=icg, b=b, k=k: s.copy(
                    hq_fm[k][:, icg * 512:(icg + 1) * 512], pd[b][:]),
                    wait=[("tP", last), ("tP", hqfm_free[0])], inc="sS")
            evs = pdbank_sS[4 + ((8 - 1) % 2)]
            evs = max(pdbank_sS[4], pdbank_sS[5])
            smx = "dA" if k == 0 else "dB"
            ldx = P.add("sync", lambda s, tg=tg, k=k: s.dma_start(
                out=x2l[k][:], in_=x2d_t[tg]),
                wait=[("vV", x2l_free[k]), ("dS", P.val("dS"))], inc=smx, n=16)
            for ho in range(2):
                b = ho
                last = None
                for ic in range(32):
                    w = []
                    if ic == 0:
                        w = [("sS", max(evs, pdbank_sS[b])),
                             ("vV", max(v_wdq, pdbank_vV[b]))]
                    last = P.add("tensor", lambda tt, ic=ic, b=b, ho=ho, k=k: tt.matmul(
                        pd[b][:, 0:512],
                        hq_fm[k][:, ic * 128:(ic + 1) * 128],
                        wdq[:, ic * 1024 + ho * 512: ic * 1024 + ho * 512 + 512],
                        start=(ic == 0), stop=(ic == 31), skip_group_check=True),
                        wait=w, inc=("tP" if ic == 31 else None))
                wv = [("tP", last), (smx, ldx)]
                ev = P.add("vector", lambda v, b=b, ho=ho, k=k: v.scalar_tensor_tensor(
                    x2l[k][:, ho * 512:(ho + 1) * 512], pd[b][:],
                    small[:, 62:63], x2l[k][:, ho * 512:(ho + 1) * 512],
                    ALU.mult, ALU.add),
                    wait=wv, inc="vV")
                pdbank_vV[b], pdbank_sS[b] = ev, 0
            hqfm_free[0] = P.val("tP")
            x2l_free[k] = P.val("vV")
            P.add("sync", lambda s, tg=tg, k=k: s.dma_start(
                out=yout_t[tg], in_=x2l[k][:]),
                wait=[("vV", P.val("vV"))], inc="dS", n=16)

    P.add("sync", None, wait=[("dS", P.val("dS")), ("dX", P.val("dX"))])
    P.add("gpsimd", None, wait=[("dS", P.val("dS"))])

    with nc.Block() as block:
        P.emit(block, sems)
    psD.close()
    dstk.close()
    top.close()
    return nc


_NC_CACHE = None


def kernel(L0_attn_hidden_states, L0_attn_Wq, L0_attn_Wk, L0_attn_Wv, L0_attn_Wo,
           L0_ffn_w_gate, L0_ffn_w_up, L0_ffn_w_down):
    global _NC_CACHE
    x = np.asarray(L0_attn_hidden_states, dtype=np.float32)
    wq = np.ascontiguousarray(np.asarray(L0_attn_Wq, np.float32).T)
    wk = np.ascontiguousarray(np.asarray(L0_attn_Wk, np.float32).T)
    wv = np.ascontiguousarray(np.asarray(L0_attn_Wv, np.float32).T)
    wo = np.ascontiguousarray(np.asarray(L0_attn_Wo, np.float32).T)
    wg = np.ascontiguousarray(np.asarray(L0_ffn_w_gate, np.float32).T)
    wu = np.ascontiguousarray(np.asarray(L0_ffn_w_up, np.float32).T)
    wd = np.ascontiguousarray(np.asarray(L0_ffn_w_down, np.float32).T)

    if _NC_CACHE is None:
        _NC_CACHE = build_nc()
    nc = _NC_CACHE

    in_maps = []
    perms = []
    for c in range(8):
        b, p = c // 2, c % 2
        tiles = list(range(p, 32, 2)) + list(range(1 - p, 32, 2))
        rows = np.concatenate([np.arange(t * 128, (t + 1) * 128) for t in tiles])
        perms.append(rows)
        xpm = np.ascontiguousarray(x[b][rows])
        mp = np.full((128, 128), float(p), np.float32)
        in_maps.append({"xp": xpm, "wqT": wq, "wkT": wk, "wvT": wv, "woT": wo,
                        "wgT": wg, "wuT": wu, "wdT": wd, "mpart": mp})

    res = run_bass_kernel_spmd(nc, in_maps, core_ids=list(range(8)))
    out = np.empty_like(x)
    for c in range(8):
        b = c // 2
        own = perms[c][:TOWN]
        out[b][own] = res.results[c]["yout"]
    return out
